# version 01 (final)
# baseline (speedup 1.0000x reference)
import numpy as np
import ml_dtypes

import concourse.bass as bass
import concourse.mybir as mybir
from concourse.bass_utils import run_bass_kernel_spmd

F32 = mybir.dt.float32
BF16 = mybir.dt.bfloat16
I32 = mybir.dt.int32
AF = mybir.ActivationFunctionType
ALU = mybir.AluOpType
AX = mybir.AxisListType
BF = ml_dtypes.bfloat16

NCORES = 8
D = 1024
TTOT = 16384
TC = TTOT // NCORES
NT = 512
PI2 = float(2 * np.pi)


class SemC:
    def __init__(self, nc, name, auto=False):
        self.h = nc.alloc_semaphore(name)
        self.n = 0
        self.auto = auto

    def inc(self, inst, k=1):
        if self.auto:
            return inst
        inst.then_inc(self.h, k)
        self.n += k
        return inst

    def raw_inc(self, inst, k=1):
        inst.then_inc(self.h, k)
        self.n += k
        return inst

    def wait(self, eng):
        if self.n:
            eng.wait_ge(self.h, self.n)


_WRAP_OPS = frozenset("""tensor_copy tensor_tensor tensor_add tensor_sub tensor_mul
tensor_max tensor_scalar tensor_single_scalar tensor_scalar_add tensor_scalar_sub
tensor_scalar_mul tensor_scalar_max tensor_scalar_min tensor_relu tensor_reduce
reduce_max reduce_sum scalar_tensor_tensor tensor_tensor_scan memset memzero
reciprocal select copy_predicated transpose stream_shuffle bn_stats bn_aggr
activation sign square sqrt mul copy add iota affine_select partition_broadcast
partition_all_reduce""".split())


class EngProxy:

    def __init__(self, eng, sem):
        self._e = eng
        self._s = sem

    def __getattr__(self, name):
        f = getattr(self._e, name)
        if name not in _WRAP_OPS:
            return f
        sem = self._s
        eng = self._e

        def g(*a, **k):
            if sem.n:
                eng.wait_ge(sem.h, sem.n)
            inst = f(*a, **k)
            sem.raw_inc(inst)
            return inst
        return g



class Ctx:

    def __init__(self, name):
        self.nc = bass.Bass("TRN2", target_bir_lowering=False)
        self.name = name
        self._consts = {}
        nc = self.nc
        self.sv = SemC(nc, "sv", auto=True)
        self.ss = SemC(nc, "ss", auto=True)
        self.sp = SemC(nc, "sp")
        self.sg = SemC(nc, "sg", auto=True)
        self._vp = EngProxy(nc.vector, self.sv)
        self._scp = EngProxy(nc.scalar, self.ss)
        self._gpp = EngProxy(nc.gpsimd, self.sg)
        self.ld = SemC(nc, "ld")
        self.st = SemC(nc, "st")
        self.ins = {}
        self.outs = {}

    @property
    def v(self):
        return self._vp

    @property
    def sc(self):
        return self._scp

    @property
    def te(self):
        return self.nc.tensor

    @property
    def gp(self):
        return self._gpp

    @property
    def sy(self):
        return self.nc.sync

    def sb(self, name, shape, dt=F32):
        return self.nc.alloc_sbuf_tensor("sb_" + name, shape, dt)

    def psum(self, name, shape, dt=F32):
        return self.nc.alloc_psum_tensor(name, shape, dt)

    def dram_in(self, name, shape, dt):
        ap = self.nc.dram_tensor(name, list(shape), dt, kind="ExternalInput").ap()
        self.ins[name] = ap
        return ap

    def dram_out(self, name, shape, dt):
        ap = self.nc.dram_tensor(name, list(shape), dt, kind="ExternalOutput").ap()
        self.outs[name] = ap
        return ap

    def const(self, name, arr):
        if name not in self._consts:
            self._consts[name] = self.nc.inline_tensor(np.ascontiguousarray(arr), name).ap()
        return self._consts[name]

    def evac_scr(self):
        if not hasattr(self, "_evs"):
            self._evs = self.sb("evac_scr", [128, NT], F32)
        return self._evs

    def cap(self, value, rows=128):
        key = float(value)
        if not hasattr(self, "_cpool"):
            self._cpool = {}
        if key not in self._cpool:
            arr = np.full((128, 1), key, np.float32)
            self._cpool[key] = self.const_sb(f"cp{len(self._cpool)}", arr, F32)
            self.ld.wait(self.sc)
            self.ld.wait(self.v)
        return self._cpool[key][0:rows, 0:1]

    def const_sb(self, name, arr, dt):
        cap = self.const(name + "_d", arr)
        t = self.sb(name, list(arr.shape), dt)
        self.ld.inc(self.sy.dma_start(out=t[:], in_=cap[:, :]), 16)
        return t

    def wait_all(self, eng):
        for s in (self.sv, self.ss, self.sp, self.sg, self.ld):
            s.wait(eng)

    def in_weight(self, name, kdim, mdim, dt=BF16):
        hbm = self.dram_in(name, [kdim, mdim], dt)
        kc = max(1, kdim // 128)
        p = min(128, kdim)
        t = self.sb(name + "_sb", [p, kc, mdim], dt)
        src = hbm[:, :].rearrange("(kc p) m -> p kc m", p=p)
        self.ld.inc(self.sy.dma_start(out=t[:], in_=src), 16)
        return t

    def in_act(self, name, ddim, T, dt):
        hbm = self.dram_in(name, [ddim, T], dt)
        dc = ddim // 128
        t = self.sb(name + "_sb", [128, dc, T], dt)
        self.ld.inc(self.sy.dma_start(out=t[:], in_=hbm[:, :].rearrange("(dc p) t -> p dc t", p=128)), 16)
        return t

    def in_vec(self, name, ddim, dt=F32):
        dc = ddim // 128
        hbm = self.dram_in(name, [128, dc], dt)
        t = self.sb(name + "_sb", [128, dc], dt)
        self.ld.inc(self.sy.dma_start(out=t[:], in_=hbm[:, :]), 16)
        return t

    def out_act(self, name, ddim, T, dt, sb_t):
        hbm = self.dram_out(name, [ddim, T], dt)
        gp = self.nc.gpsimd
        self.sv.wait(gp)
        self.sg.wait(gp)
        self.ss.wait(gp)
        self.sp.wait(gp)
        p = min(128, ddim)
        self.st.inc(gp.dma_start(
            out=hbm[:, :].rearrange("(dc p) t -> p dc t", p=p), in_=sb_t[:]), 16)

    def finish(self):
        self.st.wait(self.gp)


def vec_fm(vec):
    return np.ascontiguousarray(np.asarray(vec, np.float32).reshape(-1, 128).T)



def ones_consts(ctx):
    oc = ctx.const_sb("ones_col", np.ones((128, 1), dtype=BF), BF16)
    orow = ctx.const_sb("ones_row", np.ones((1, 128), dtype=BF), BF16)
    return oc, orow


def emit_proj(ctx, psc, w_sb, in_sb, kcs, mcs, T, evac, m0=0, nt=NT):
    nc = ctx.nc
    tile = psc
    for mc in range(mcs):
        for t0 in range(0, T, nt):
            n = min(nt, T - t0)
            ctx.sv.wait(nc.tensor)
            ctx.ss.wait(nc.tensor)
            for i, kc in enumerate(kcs):
                inst = nc.tensor.matmul(
                    tile[:, 0:n],
                    w_sb[:, kc, m0 + mc * 128: m0 + (mc + 1) * 128],
                    in_sb[:, kc, t0:t0 + n],
                    start=(i == 0), stop=(i == len(kcs) - 1))
            ctx.sp.inc(inst)
            ctx.sp.wait(nc.vector)
            ctx.sp.wait(nc.scalar)
            evac(tile, mc, t0, n)


def emit_rows_recip_stable(ctx, out_row, in_row, n, eps):
    v = ctx.v
    v.tensor_scalar_add(out_row[0:1, 0:n], in_row[0:1, 0:n], eps)
    ctx.sv.inc(v.reciprocal(out_row[0:1, 0:n], out_row[0:1, 0:n]))


def emit_ln(ctx, T, DC, in_f32, apply_cb, gamma=None, beta=None, eps=1e-5,
            name="ln", oc=None, orow=None, ps_small=None, ps_bc=None, in_loader=None):
    nc = ctx.nc
    v, sc, te = ctx.v, ctx.sc, ctx.te
    D_ = DC * 128
    if not hasattr(ctx, "_ln_scr"):
        ctx._ln_scr = dict(
            rows=ctx.sb("lnS_rows", [1, 4 * NT], F32),
            rows_bf=ctx.sb("lnS_rows_bf", [1, 2 * NT], BF16),
            tb=ctx.sb("lnS_tb", [128, 8, NT], BF16),
            sq=ctx.sb("lnS_sq", [128, 8, NT], BF16),
            xs=ctx.sb("lnS_xs", [128, NT], F32),
            bcs=ctx.sb("lnS_bcs", [128, 2 * NT], F32),
        )
    S = ctx._ln_scr
    rows, rows_bf, tb, sq, xs = S["rows"], S["rows_bf"], S["tb"], S["sq"], S["xs"]
    bcs = S["bcs"]
    for t0 in range(0, T, NT):
        n = min(NT, T - t0)
        mu = rows[0:1, 0:n]
        msq = rows[0:1, NT:NT + n]
        A = rows[0:1, 2 * NT:2 * NT + n]
        B = rows[0:1, 3 * NT:3 * NT + n]
        if in_loader is not None:
            in_f32 = in_loader(t0, n)
        ctx.sp.wait(v)
        for c in range(DC):
            v.tensor_copy(tb[:, c, 0:n], in_f32[:, c, t0:t0 + n])
            ins = v.tensor_tensor(sq[:, c, 0:n], tb[:, c, 0:n], tb[:, c, 0:n], ALU.mult)
        ctx.sv.inc(ins)
        ctx.sv.wait(te)
        for j, src_ in enumerate((tb, sq)):
            ctx.ss.wait(te)
            for c in range(DC):
                inst = te.matmul(ps_small[0:1, 0:n], oc[:, 0:1], src_[:, c, 0:n],
                                 start=(c == 0), stop=(c == DC - 1))
            ctx.sp.inc(inst)
            ctx.sp.wait(sc)
            dst = mu if j == 0 else msq
            ctx.ss.inc(sc.activation(dst, ps_small[0:1, 0:n], AF.Copy, scale=1.0 / D_))
        ctx.ss.wait(v)
        v.tensor_tensor(A, mu, mu, ALU.mult)
        ctx.sv.inc(v.tensor_sub(A, msq, A))
        ctx.sv.wait(sc)
        sc.activation(A, A, AF.Ln, bias=ctx.cap(eps, rows=1))
        ctx.ss.inc(sc.activation(A, A, AF.Exp, scale=-0.5))
        ctx.ss.wait(v)
        v.tensor_tensor(B, mu, A, ALU.mult)
        v.tensor_scalar_mul(B, B, -1.0)
        v.tensor_copy(rows_bf[0:1, 0:n], A)
        ctx.sv.inc(v.tensor_copy(rows_bf[0:1, NT:NT + n], B))
        ctx.sv.wait(te)
        te.matmul(ps_bc[:, 0:n], orow[0:1, :], rows_bf[0:1, 0:n], start=True, stop=True)
        ctx.sp.inc(te.matmul(ps_bc[:, NT:NT + n], orow[0:1, :], rows_bf[0:1, NT:NT + n],
                             start=True, stop=True))
        ctx.sp.wait(v)
        v.tensor_copy(bcs[:, 0:n], ps_bc[:, 0:n])
        ctx.sv.inc(v.tensor_copy(bcs[:, NT:NT + n], ps_bc[:, NT:NT + n]))
        for c in range(DC):
            v.tensor_tensor(xs[:, 0:n], in_f32[:, c, t0:t0 + n], bcs[:, 0:n], ALU.mult)
            ctx.sv.inc(v.tensor_tensor(xs[:, 0:n], xs[:, 0:n], bcs[:, NT:NT + n], ALU.add))
            apply_cb(c, t0, n, xs[:, 0:n])


def emit_elu1_evac(ctx, out_bf, scr):
    v, sc = ctx.v, ctx.sc

    def cb(tile, mc, t0, n):
        evs = ctx.evac_scr()
        v.tensor_copy(evs[:, 0:n], tile[:, 0:n])
        v.tensor_scalar_min(scr[:, 0:n], evs[:, 0:n], 0.0)
        ctx.sv.inc(v.tensor_scalar_max(evs[:, 0:n], evs[:, 0:n], 0.0))
        ctx.sv.wait(sc)
        ctx.ss.inc(sc.activation(scr[:, 0:n], scr[:, 0:n], AF.Exp))
        ctx.ss.wait(v)
        ctx.sv.inc(v.tensor_tensor(out_bf[:, mc, t0:t0 + n], scr[:, 0:n],
                                   evs[:, 0:n], ALU.add))
    return cb


def emit_copy_evac(ctx, out_sb, func=AF.Copy, bias=None, scale=1.0):
    sc = ctx.sc

    def cb(tile, mc, t0, n):
        if func == AF.Copy and bias is not None:
            evs = ctx.evac_scr()
            ctx.v.tensor_copy(evs[:, 0:n], tile[:, 0:n])
            ctx.sv.inc(ctx.v.tensor_scalar(out_sb[:, mc, t0:t0 + n], evs[:, 0:n],
                                           bias[:, mc:mc + 1], None, ALU.add))
            return
        b = bias[:, mc:mc + 1] if bias is not None else 0.0
        ctx.ss.inc(sc.activation(out_sb[:, mc, t0:t0 + n], tile[:, 0:n], func,
                                 bias=b, scale=scale))
    return cb



def build_prep(n_castcols, qshapes):
    ctx = Ctx("prep")
    nc = ctx.nc
    cin = ctx.dram_in("cast_in", [128, n_castcols], F32)
    cout = ctx.dram_out("cast_out", [128, n_castcols], BF16)
    qins = [ctx.dram_in(f"q_in{i}", [r, c], F32) for i, (r, c) in enumerate(qshapes)]
    qouts = [ctx.dram_out(f"q_out{i}", [r, c], BF16) for i, (r, c) in enumerate(qshapes)]
    CW = 16384
    a = ctx.sb("a", [128, CW], F32)
    b = ctx.sb("b", [128, CW], BF16)
    qt = dict(
        x=ctx.sb("qx", [128, 4096], F32), xa=ctx.sb("qxa", [128, 4096], F32),
        s=ctx.sb("qs", [128, 256], F32), rs=ctx.sb("qrs", [128, 256], F32),
        lo=ctx.sb("qlo", [128, 4096], F32), ti=ctx.sb("qti", [128, 4096], I32),
        t1=ctx.sb("qt1", [128, 4096], F32), sg=ctx.sb("qsg", [128, 4096], F32),
        o=ctx.sb("qo", [128, 4096], BF16),
    )
    with nc.Block():
        for c0 in range(0, n_castcols, CW):
            n = min(CW, n_castcols - c0)
            ctx.sv.wait(nc.sync)
            ctx.ld.inc(nc.sync.dma_start(out=a[:, 0:n], in_=cin[:, c0:c0 + n]), 16)
            ctx.ld.wait(ctx.v)
            ctx.st.wait(ctx.v)
            ctx.sv.inc(ctx.v.tensor_copy(b[:, 0:n], a[:, 0:n]))
            ctx.sv.wait(ctx.gp)
            ctx.st.inc(ctx.gp.dma_start(out=cout[:, c0:c0 + n], in_=b[:, 0:n]), 16)
        for qi, (r, c) in enumerate(qshapes):
            for r0 in range(0, r, 128):
                p = min(128, r - r0)
                ctx.sv.wait(nc.sync)
                ctx.ss.wait(nc.sync)
                ctx.ld.inc(nc.sync.dma_start(out=qt["x"][0:p, 0:c], in_=qins[qi][r0:r0 + p, :]), 16)
                ctx.ld.wait(ctx.v)
                ctx.ld.wait(ctx.sc)
                ctx.st.wait(ctx.v)
                emit_fp4(ctx, qt, p, c)
                ctx.sv.wait(ctx.gp)
                ctx.st.inc(ctx.gp.dma_start(out=qouts[qi][r0:r0 + p, :], in_=qt["o"][0:p, 0:c]), 16)
        ctx.finish()
    return ctx


def emit_fp4(ctx, t, p, c):
    v, sc = ctx.v, ctx.sc
    nb = c // 16
    x = t["x"][0:p, 0:c]
    xv = x.rearrange("p (a b) -> p a b", b=16)
    v.tensor_reduce(t["s"][0:p, 0:nb], xv, AX.X, ALU.max, apply_absolute_value=True)
    v.tensor_scalar(t["s"][0:p, 0:nb], t["s"][0:p, 0:nb], 1.0 / 6.0, 1e-12, ALU.mult, ALU.max)
    ctx.sv.inc(v.reciprocal(t["rs"][0:p, 0:nb], t["s"][0:p, 0:nb]))
    ctx.sv.wait(sc)
    ctx.ss.inc(sc.activation(t["xa"][0:p, 0:c], x, AF.Abs))
    ctx.ss.wait(v)
    rsb = t["rs"][0:p, 0:nb].unsqueeze(2).to_broadcast([p, nb, 16])
    xav = t["xa"][0:p, 0:c].rearrange("p (a b) -> p a b", b=16)
    v.tensor_tensor(xav, xav, rsb, ALU.mult)
    xa = t["xa"][0:p, 0:c]
    v.tensor_scalar(t["lo"][0:p, 0:c], xa, 2.0, 4.0, ALU.mult, ALU.min)
    v.tensor_copy(t["ti"][0:p, 0:c], t["lo"][0:p, 0:c])
    v.tensor_copy(t["lo"][0:p, 0:c], t["ti"][0:p, 0:c])
    ctx.sv.inc(v.tensor_scalar_mul(t["lo"][0:p, 0:c], t["lo"][0:p, 0:c], 0.5))
    for thr, wgt in ((2.5, 1.0), (3.5, 1.0), (5.0, 2.0)):
        v.tensor_single_scalar(t["t1"][0:p, 0:c], xa, thr, ALU.is_gt)
        v.scalar_tensor_tensor(t["lo"][0:p, 0:c], t["t1"][0:p, 0:c], wgt,
                               t["lo"][0:p, 0:c], ALU.mult, ALU.add)
    ctx.sv.wait(sc)
    ctx.ss.inc(sc.sign(t["sg"][0:p, 0:c], x))
    ctx.ss.wait(v)
    v.tensor_tensor(t["lo"][0:p, 0:c], t["lo"][0:p, 0:c], t["sg"][0:p, 0:c], ALU.mult)
    sbv = t["s"][0:p, 0:nb].unsqueeze(2).to_broadcast([p, nb, 16])
    lov = t["lo"][0:p, 0:c].rearrange("p (a b) -> p a b", b=16)
    ov = t["o"][0:p, 0:c].rearrange("p (a b) -> p a b", b=16)
    ctx.sv.inc(v.tensor_tensor(ov, lov, sbv, ALU.mult))



def build_l0a():
    ctx = Ctx("l0a")
    nc = ctx.nc
    with nc.Block():
        oc, orow = ones_consts(ctx)
        xf = ctx.in_act("x", D, TC, F32)
        g1 = ctx.in_vec("g1", D)
        be1 = ctx.in_vec("be1", D)
        wbuf = ctx.sb("wbuf", [128, 8, 1024], BF16)
        wq_h = ctx.dram_in("wqT", [D, D], BF16)
        wk_h = ctx.dram_in("wkT", [D, D], BF16)
        wv_h = ctx.dram_in("wvT", [D, D], BF16)
        x2b = ctx.sb("x2b", [128, 8, TC], BF16)
        outb = ctx.sb("outb", [128, 8, TC], BF16)
        scr = ctx.sb("scr", [128, NT], F32)
        ps_small = ctx.psum("ps_small", [1, NT])
        ps_bc = ctx.psum("ps_bc", [128, 2 * NT])
        ps_mm = ctx.psum("ps_mm", [128, NT])
        ctx.ld.wait(ctx.v)
        ctx.ld.wait(ctx.sc)
        ctx.ld.wait(ctx.te)

        def cb1(c, t0, n, xs):
            ctx.sv.inc(ctx.v.tensor_copy(xf[:, c, t0:t0 + n], xs))
        emit_ln(ctx, TC, 8, xf, cb1, oc=oc, orow=orow, ps_small=ps_small, ps_bc=ps_bc)
        ctx.out_act("xn", D, TC, F32, xf)

        def cb2(c, t0, n, xs):
            ctx.sv.inc(ctx.v.tensor_scalar(x2b[:, c, t0:t0 + n], xs,
                                           g1[:, c:c + 1], be1[:, c:c + 1],
                                           ALU.mult, ALU.add))
        emit_ln(ctx, TC, 8, xf, cb2, oc=oc, orow=orow, ps_small=ps_small, ps_bc=ps_bc)

        for wname, w_h, elu in (("Q", wq_h, True), ("K", wk_h, True), ("V", wv_h, False)):
            ctx.sp.wait(ctx.sy)
            ctx.ld.inc(ctx.sy.dma_start(
                out=wbuf[:], in_=w_h[:, :].rearrange("(kc p) m -> p kc m", p=128)), 16)
            ctx.ld.wait(ctx.te)
            ctx.st.wait(ctx.sc)
            ctx.st.wait(ctx.v)
            if elu:
                cb = emit_elu1_evac(ctx, outb, scr)
            else:
                cb = emit_copy_evac(ctx, outb)
            emit_proj(ctx, ps_mm, wbuf, x2b, list(range(8)), 8, TC, cb)
            ctx.out_act(wname, D, TC, BF16, outb)
        ctx.finish()
    return ctx



def build_l0b():
    L = 8192
    ctx = Ctx("l0b")
    nc = ctx.nc
    with nc.Block():
        oc, orow = ones_consts(ctx)
        ps_kv = ctx.psum("ps_kv", [128, 129])
        ps_num = ctx.psum("ps_num", [128, NT])
        ps_den = ctx.psum("ps_den", [1, NT])
        ps_bc = ctx.psum("ps_bcast", [128, NT])
        rowbuf = ctx.sb("rowbuf", [1, NT], F32)
        rowbf = ctx.sb("rowbf", [1, NT], BF16)
        rb = ctx.sb("rb", [128, NT], F32)
        numb = ctx.sb("numb", [128, NT], F32)
        kvb = ctx.sb("kvb", [128, 129], BF16)
        for p in range(2):
            qf = ctx.in_act(f"q{p}", 128, L, BF16)
            kt = ctx.in_weight(f"kt{p}", L, 128)
            vt = ctx.in_weight(f"vt{p}", L, 128)
            ob = ctx.sb(f"ob{p}", [128, 1, L], BF16)
            ctx.ld.wait(ctx.te)
            ctx.ss.wait(ctx.te)
            ctx.sp.wait(ctx.te)
            for kc in range(64):
                te_i = ctx.te.matmul(ps_kv[:, 0:128], kt[:, kc, :], vt[:, kc, :],
                                     start=(kc == 0), stop=(kc == 63))
            for kc in range(64):
                te_i = ctx.te.matmul(ps_kv[:, 128:129], kt[:, kc, :], oc[:, 0:1],
                                     start=(kc == 0), stop=(kc == 63))
            ctx.sp.inc(te_i)
            ctx.sp.wait(ctx.sc)
            ctx.ss.inc(ctx.sc.activation(kvb[:, :], ps_kv[:, :], AF.Copy))
            for t0 in range(0, L, NT):
                n = NT
                ctx.ss.wait(ctx.te)
                ctx.sv.wait(ctx.te)
                ctx.te.matmul(ps_num[:, 0:n], kvb[:, 0:128], qf[:, 0, t0:t0 + n],
                              start=True, stop=True)
                ctx.sp.inc(ctx.te.matmul(ps_den[0:1, 0:n], kvb[:, 128:129],
                                         qf[:, 0, t0:t0 + n], start=True, stop=True))
                ctx.sp.wait(ctx.v)
                ctx.v.tensor_copy(rowbuf[0:1, 0:n], ps_den[0:1, 0:n])
                ctx.v.tensor_scalar_add(rowbuf[0:1, 0:n], rowbuf[0:1, 0:n], 1e-6)
                ctx.v.reciprocal(rowbuf[0:1, 0:n], rowbuf[0:1, 0:n])
                ctx.sv.inc(ctx.v.tensor_copy(rowbf[0:1, 0:n], rowbuf[0:1, 0:n]))
                ctx.sv.wait(ctx.te)
                ctx.sp.inc(ctx.te.matmul(ps_bc[:, 0:n], orow[0:1, :], rowbf[0:1, 0:n],
                                         start=True, stop=True))
                ctx.sp.wait(ctx.sc)
                ctx.ss.inc(ctx.sc.activation(rb[:, 0:n], ps_bc[:, 0:n], AF.Copy))
                ctx.ss.wait(ctx.v)
                ctx.v.tensor_copy(numb[:, 0:n], ps_num[:, 0:n])
                ctx.sv.inc(ctx.v.tensor_tensor(ob[:, 0, t0:t0 + n], numb[:, 0:n],
                                               rb[:, 0:n], ALU.mult))
            ctx.out_act(f"msg{p}", 128, L, BF16, ob)
        ctx.finish()
    return ctx



def build_l0c1():
    ctx = Ctx("l0c1")
    nc = ctx.nc
    with nc.Block():
        oc, orow = ones_consts(ctx)
        xn = ctx.in_act("xn", D, TC, F32)
        msg = ctx.in_act("msg", D, TC, BF16)
        g2 = ctx.in_vec("g2", D)
        be2 = ctx.in_vec("be2", D)
        x3b = ctx.sb("x3b", [128, 8, TC], BF16)
        ps_small = ctx.psum("ps_small", [1, NT])
        ps_bc = ctx.psum("ps_bc", [128, 2 * NT])
        ctx.ld.wait(ctx.v)
        for c in range(8):
            ins = ctx.v.tensor_tensor(xn[:, c, :], xn[:, c, :], msg[:, c, :], ALU.add)
        ctx.sv.inc(ins)
        ctx.out_act("x0", D, TC, F32, xn)

        def cb(c, t0, n, xs):
            ctx.sv.inc(ctx.v.tensor_scalar(x3b[:, c, t0:t0 + n], xs,
                                           g2[:, c:c + 1], be2[:, c:c + 1],
                                           ALU.mult, ALU.add))
        emit_ln(ctx, TC, 8, xn, cb, oc=oc, orow=orow, ps_small=ps_small, ps_bc=ps_bc)
        ctx.out_act("x3b", D, TC, BF16, x3b)
        ctx.finish()
    return ctx



def build_l0c2():
    ctx = Ctx("l0c2")
    nc = ctx.nc
    with nc.Block():
        x3b = ctx.in_act("x3b", D, TC, BF16)
        w1 = ctx.in_weight("w1T", D, 4096)
        b1 = ctx.in_vec("b1", 4096)
        h_h = ctx.dram_out("h", [4096, TC], BF16)
        ht = ctx.sb("ht", [128, 32, NT], BF16)
        ps_mm = ctx.psum("ps_mm", [128, NT])
        ctx.ld.wait(ctx.te)
        ctx.ld.wait(ctx.sc)
        for tt in range(TC // NT):
            t0 = tt * NT
            ctx.st.wait(ctx.sc)
            for mc in range(32):
                ctx.ss.wait(ctx.te)
                for i in range(8):
                    inst = ctx.te.matmul(ps_mm[:, :], w1[:, i, mc * 128:(mc + 1) * 128],
                                         x3b[:, i, t0:t0 + NT], start=(i == 0), stop=(i == 7))
                ctx.sp.inc(inst)
                ctx.sp.wait(ctx.sc)
                ctx.ss.inc(ctx.sc.activation(ht[:, mc, :], ps_mm[:, :], AF.Gelu,
                                             bias=b1[:, mc:mc + 1]))
            ctx.ss.wait(ctx.gp)
            ctx.st.inc(ctx.gp.dma_start(
                out=h_h[:, t0:t0 + NT].rearrange("(dc p) t -> p dc t", p=128),
                in_=ht[:]), 16)
        ctx.finish()
    return ctx



def build_l0c3():
    ctx = Ctx("l0c3")
    nc = ctx.nc
    with nc.Block():
        h_h = ctx.dram_in("h", [4096, TC], BF16)
        w2 = ctx.in_weight("w2T", 4096, D)
        b2 = ctx.in_vec("b2", D)
        x0 = ctx.in_act("x0", D, TC, F32)
        s_h = ctx.dram_out("s", [D, TC], F32)
        ht = ctx.sb("ht", [128, 32, NT], BF16)
        st_ = ctx.sb("st_", [128, 8, NT], F32)
        ps_mm = ctx.psum("ps_mm", [128, NT])
        hld = SemC(nc, "hld")
        ctx.ld.wait(ctx.te)
        ctx.ld.wait(ctx.sc)
        ctx.ld.wait(ctx.v)
        for tt in range(TC // NT):
            t0 = tt * NT
            ctx.sp.wait(ctx.sy)
            hld.inc(ctx.sy.dma_start(
                out=ht[:], in_=h_h[:, t0:t0 + NT].rearrange("(dc p) t -> p dc t", p=128)), 16)
            hld.wait(ctx.te)
            ctx.st.wait(ctx.v)
            for mc in range(8):
                ctx.sv.wait(ctx.te)
                for i in range(32):
                    inst = ctx.te.matmul(ps_mm[:, :], w2[:, i, mc * 128:(mc + 1) * 128],
                                         ht[:, i, :], start=(i == 0), stop=(i == 31))
                ctx.sp.inc(inst)
                ctx.sp.wait(ctx.v)
                ctx.v.tensor_copy(st_[:, mc, :], ps_mm[:, :])
                ctx.sv.inc(ctx.v.tensor_tensor(st_[:, mc, :], st_[:, mc, :],
                                               x0[:, mc, t0:t0 + NT], ALU.add))
            for mc in range(8):
                ins = ctx.v.tensor_scalar(st_[:, mc, :], st_[:, mc, :],
                                          b2[:, mc:mc + 1], None, ALU.add)
            ctx.sv.inc(ins)
            ctx.sv.wait(ctx.gp)
            ctx.st.inc(ctx.gp.dma_start(
                out=s_h[:, t0:t0 + NT].rearrange("(dc p) t -> p dc t", p=128),
                in_=st_[:]), 16)
        ctx.finish()
    return ctx



def build_l0d1():
    ctx = Ctx("l0d1")
    nc = ctx.nc
    with nc.Block():
        xnf = ctx.in_act("xn", D, TC, F32)
        w_in = ctx.in_weight("W_in", D, D)
        b_in = ctx.in_vec("b_in", D)
        wg = ctx.in_weight("W_gate", D, 8)
        bgd = ctx.dram_in("b_gate", [128, 8], F32)
        xn = ctx.sb("xnb", [128, 8, TC], BF16)
        tb = ctx.sb("tb", [128, 8, TC], BF16)
        route_t = ctx.dram_out("route_tok", [TC, 8], F32)
        ps_mm = ctx.psum("ps_mm", [128, NT])
        ps_g = ctx.psum("ps_g", [128, 8])
        lg = ctx.sb("lg", [128, 8], F32)
        m1 = ctx.sb("m1", [128, 1], F32)
        m2 = ctx.sb("m2", [128, 1], F32)
        e2 = ctx.sb("e2", [128, 1], F32)
        r1 = ctx.sb("r1", [128, 1], F32)
        msk = ctx.sb("msk", [128, 8], F32)
        msk2 = ctx.sb("msk2", [128, 8], F32)
        rt = ctx.sb("rt", [128, 8], F32)
        bgb = ctx.sb("bgb", [128, 8], F32)
        ctx.ld.inc(ctx.sy.dma_start(out=bgb[:, :], in_=bgd[:, :]), 16)
        ctx.ld.wait(ctx.te)
        ctx.ld.wait(ctx.sc)
        ctx.ld.wait(ctx.v)
        for c in range(8):
            ins = ctx.v.tensor_copy(xn[:, c, :], xnf[:, c, :])
        ctx.sv.inc(ins)
        cb = emit_copy_evac(ctx, tb, bias=b_in)
        emit_proj(ctx, ps_mm, w_in, xn, list(range(8)), 8, TC, cb)
        ctx.out_act("t", D, TC, BF16, tb)
        for i in range(TC // 128):
            t0 = i * 128
            ctx.sv.wait(ctx.te)
            ctx.ss.wait(ctx.te)
            for kc in range(8):
                inst = ctx.te.matmul(ps_g[:, :], tb[:, kc, t0:t0 + 128], wg[:, kc, :],
                                     start=(kc == 0), stop=(kc == 7))
            ctx.sp.inc(inst)
            ctx.sp.wait(ctx.v)
            v = ctx.v
            v.tensor_copy(lg[:, :], ps_g[:, :])
            v.tensor_tensor(lg[:, :], lg[:, :], bgb[:, :], ALU.add)
            v.tensor_reduce(m1[:, :], lg[:, :], AX.X, ALU.max)
            v.tensor_scalar(msk[:, :], lg[:, :], m1[:, 0:1], None, ALU.is_ge)
            v.scalar_tensor_tensor(msk2[:, :], msk[:, :], -1e30, lg[:, :], ALU.mult, ALU.add)
            v.tensor_reduce(m2[:, :], msk2[:, :], AX.X, ALU.max)
            ctx.sv.inc(v.tensor_scalar(msk2[:, :], msk2[:, :], m2[:, 0:1], None, ALU.is_ge))
            ctx.sv.wait(ctx.sc)
            ctx.ss.inc(ctx.sc.activation(e2[:, :], m2[:, :], AF.Exp, bias=m1[:, 0:1], scale=-1.0))
            ctx.ss.wait(ctx.v)
            v.tensor_scalar_add(r1[:, :], e2[:, :], 1.0)
            v.reciprocal(r1[:, :], r1[:, :])
            v.tensor_tensor(e2[:, :], e2[:, :], r1[:, :], ALU.mult)
            v.tensor_scalar(rt[:, :], msk[:, :], e2[:, 0:1], None, ALU.mult)
            ctx.sv.inc(v.scalar_tensor_tensor(rt[:, :], msk2[:, :], r1[:, 0:1], rt[:, :],
                                              ALU.mult, ALU.add))
            ctx.sv.wait(ctx.gp)
            ctx.st.inc(ctx.gp.dma_start(out=route_t[t0:t0 + 128, :], in_=rt[:, :]), 16)
            ctx.st.wait(ctx.v)
        ctx.finish()
    return ctx



def build_l0d2():
    ctx = Ctx("l0d2")
    nc = ctx.nc
    NE = 8
    with nc.Block():
        tb = ctx.in_act("t", D, TC, BF16)
        w1d = ctx.dram_in("W1", [NE * D, 4096], BF16)
        w2d = ctx.dram_in("W2", [NE * 4096, D], BF16)
        b1e = ctx.in_vec("b1e", 4096 * NE)
        rtd = ctx.dram_in("route_tok", [TC, 8], F32)
        rfmd = ctx.dram_in("route_fm", [8, TC], F32)
        b2e = ctx.in_weight("b2e", 8, D)
        w1 = ctx.sb("w1", [128, 8, 2048], BF16)
        w2 = ctx.sb("w2", [128, 16, D], BF16)
        mid = ctx.sb("mid", [128, 16, NT], BF16)
        acc = ctx.sb("acc", [128, 16, D], BF16)
        rsb = ctx.sb("rsb", [128, 16, 8], F32)
        rfm_f = ctx.sb("rfm_f", [8, TC], F32)
        rfm = ctx.sb("rfm", [8, TC], BF16)
        ps_mm = ctx.psum("ps_mm", [128, NT])
        ps_e = ctx.psum("ps_e", [128, NT])
        ctx.ld.inc(ctx.sy.dma_start(out=rsb[:], in_=rtd[:, :].rearrange("(i p) e -> p i e", p=128)), 16)
        ctx.ld.inc(ctx.sy.dma_start(out=rfm_f[:], in_=rfmd[:, :]), 16)
        ctx.ld.wait(ctx.v)
        ctx.sv.inc(ctx.v.tensor_copy(rfm[:, :], rfm_f[:, :]))
        for c in range(16):
            ins = ctx.v.memset(acc[:, c, :], 0.0)
        ctx.sv.inc(ins)
        wsem = SemC(nc, "wsem")
        first = True
        for e in range(NE):
            for half in range(2):
                ctx.sp.wait(ctx.sy)
                wsem.inc(ctx.sy.dma_start(
                    out=w1[:],
                    in_=w1d[e * D:(e + 1) * D, half * 2048:(half + 1) * 2048]
                    .rearrange("(kc p) m -> p kc m", p=128)), 16)
                wsem.inc(ctx.sy.dma_start(
                    out=w2[:],
                    in_=w2d[e * 4096 + half * 2048:e * 4096 + (half + 1) * 2048, :]
                    .rearrange("(kc p) m -> p kc m", p=128)), 16)
                wsem.wait(ctx.te)
                if first:
                    ctx.ld.wait(ctx.te)
                    ctx.ld.wait(ctx.sc)
                    ctx.ld.wait(ctx.v)
                    first = False
                for tt in range(TC // NT):
                    t0 = tt * NT
                    for mc in range(16):
                        gmc = half * 16 + mc
                        ctx.ss.wait(ctx.te)
                        for i in range(8):
                            inst = ctx.te.matmul(ps_mm[:, :], w1[:, i, mc * 128:(mc + 1) * 128],
                                                 tb[:, i, t0:t0 + NT], start=(i == 0), stop=(i == 7))
                        ctx.sp.inc(inst)
                        ctx.sp.wait(ctx.sc)
                        ctx.ss.inc(ctx.sc.activation(mid[:, mc, :], ps_mm[:, :], AF.Relu,
                                                     bias=b1e[:, e * 32 + gmc:e * 32 + gmc + 1]))
                    for ts in range(4):
                        tok = tt * 4 + ts
                        for f in range(2):
                            ctx.sv.wait(ctx.te)
                            ctx.ss.wait(ctx.te)
                            for i in range(16):
                                inst = ctx.te.matmul(
                                    ps_e[:, :], mid[:, i, ts * 128:(ts + 1) * 128],
                                    w2[:, i, f * NT:(f + 1) * NT], start=(i == 0), stop=(i == 15))
                            ctx.sp.inc(inst)
                            ctx.sp.wait(ctx.v)
                            evs = ctx.evac_scr()
                            ctx.v.tensor_copy(evs[:, :], ps_e[:, :])
                            ctx.sv.inc(ctx.v.scalar_tensor_tensor(
                                acc[:, tok, f * NT:(f + 1) * NT], evs[:, :],
                                rsb[:, tok, e:e + 1], acc[:, tok, f * NT:(f + 1) * NT],
                                ALU.mult, ALU.add))
        for tok in range(16):
            for f in range(2):
                ctx.sv.wait(ctx.te)
                inst = ctx.te.matmul(ps_e[:, :], rfm[0:8, tok * 128:(tok + 1) * 128],
                                     b2e[0:8, 0, f * NT:(f + 1) * NT], start=True, stop=True)
                ctx.sp.inc(inst)
                ctx.sp.wait(ctx.v)
                evs = ctx.evac_scr()
                ctx.v.tensor_copy(evs[:, :], ps_e[:, :])
                ctx.sv.inc(ctx.v.tensor_tensor(
                    acc[:, tok, f * NT:(f + 1) * NT], evs[:, :],
                    acc[:, tok, f * NT:(f + 1) * NT], ALU.add))
        moe = ctx.dram_out("moe_tok", [TC, D], BF16)
        ctx.sv.wait(ctx.gp)
        ctx.st.inc(ctx.gp.dma_start(out=moe[:, :].rearrange("(i p) f -> p i f", p=128),
                                    in_=acc[:]), 16)
        ctx.finish()
    return ctx



def build_l0d3():
    ctx = Ctx("l0d3")
    nc = ctx.nc
    with nc.Block():
        moeT = ctx.in_act("moeT", D, TC, BF16)
        wo = ctx.in_weight("W_out", D, D)
        bo = ctx.in_vec("b_out", D)
        x = ctx.in_act("x", D, TC, F32)
        s = ctx.in_act("s", D, TC, F32)
        x1h = ctx.dram_out("x1", [D, TC], F32)
        x1t = ctx.sb("x1t", [128, 8, NT], F32)
        scr = ctx.sb("scr", [128, NT], F32)
        ps_mm = ctx.psum("ps_mm", [128, NT])
        ctx.ld.wait(ctx.te)
        ctx.ld.wait(ctx.sc)
        ctx.ld.wait(ctx.v)
        for tt in range(TC // NT):
            t0 = tt * NT
            ctx.st.wait(ctx.v)
            for mc in range(8):
                ctx.sv.wait(ctx.te)
                ctx.ss.wait(ctx.te)
                for i in range(8):
                    inst = ctx.te.matmul(ps_mm[:, :], wo[:, i, mc * 128:(mc + 1) * 128],
                                         moeT[:, i, t0:t0 + NT], start=(i == 0), stop=(i == 7))
                ctx.sp.inc(inst)
                ctx.sp.wait(ctx.sc)
                ctx.ss.inc(ctx.sc.activation(scr[:, :], ps_mm[:, :], AF.Silu,
                                             bias=bo[:, mc:mc + 1]))
                ctx.ss.wait(ctx.v)
                ctx.v.tensor_tensor(scr[:, :], scr[:, :], s[:, mc, t0:t0 + NT], ALU.mult)
                ctx.sv.inc(ctx.v.tensor_tensor(x1t[:, mc, :], scr[:, :],
                                               x[:, mc, t0:t0 + NT], ALU.add))
            ctx.sv.wait(ctx.gp)
            ctx.st.inc(ctx.gp.dma_start(
                out=x1h[:, t0:t0 + NT].rearrange("(dc p) t -> p dc t", p=128),
                in_=x1t[:]), 16)
        ctx.finish()
    return ctx



def build_l1a(stages=("ln", "proj", "blog")):
    ctx = Ctx("l1a")
    nc = ctx.nc
    with nc.Block():
        oc, orow = ones_consts(ctx)
        x1 = ctx.in_act("x1", D, TC, F32)
        ws = {}
        for nm, md in (("WqT", 512), ("WkT", 512), ("WvT", D), ("WaT", D), ("WgT", D)):
            ws[nm] = ctx.dram_in(nm, [D, md], BF16)
        wbT = ctx.in_weight("WbT", D, 8)
        ba = ctx.in_vec("ba", D)
        bbd = ctx.dram_in("bb", [8, 1], F32)
        bb = ctx.sb("bb_sb", [8, 1], F32)
        ctx.ld.inc(ctx.sy.dma_start(out=bb[:], in_=bbd[:, :]), 16)
        xn1b = ctx.sb("xn1b", [128, 8, TC], BF16)
        wbuf = ctx.sb("wbuf", [128, 8, 1024], BF16)
        wbuf2 = ctx.sb("wbuf2", [128, 8, 512], BF16)
        outb = ctx.sb("outb", [128, 8, TC], BF16)
        ps_small = ctx.psum("ps_small", [1, NT])
        ps_bc = ctx.psum("ps_bc", [128, 2 * NT])
        ps_mm = ctx.psum("ps_mm", [128, NT])
        ps_b = ctx.psum("ps_b", [8, NT])
        ctx.ld.wait(ctx.te)
        ctx.ld.wait(ctx.sc)
        ctx.ld.wait(ctx.v)

        def cb(c, t0, n, xs):
            ctx.sv.inc(ctx.v.tensor_copy(xn1b[:, c, t0:t0 + n], xs))
        emit_ln(ctx, TC, 8, x1, cb, oc=oc, orow=orow, ps_small=ps_small, ps_bc=ps_bc)
        ctx.out_act("xn1b", D, TC, BF16, xn1b)

        wld = SemC(nc, "wld")
        _pl = (
                ("qpre", "WqT", 4, AF.Copy, None), ("kpre", "WkT", 4, AF.Copy, None),
                ("vpre", "WvT", 8, AF.Copy, None), ("alog", "WaT", 8, AF.Copy, ba),
                ("gate", "WgT", 8, AF.Sigmoid, None))
        nproj = sum(1 for s in stages if s.startswith("proj"))
        if "proj" in stages:
            plist = _pl
        elif nproj:
            plist = _pl[:int([s for s in stages if s.startswith("proj")][0][4:] or 5)]
        else:
            plist = ()
        _old = (
                ("qpre", "WqT", 4, AF.Copy, None), ("kpre", "WkT", 4, AF.Copy, None),
                ("vpre", "WvT", 8, AF.Copy, None), ("alog", "WaT", 8, AF.Copy, ba),
                ("gate", "WgT", 8, AF.Sigmoid, None))
        del _old
        for name, wnm, mcs, func, bias in plist:
            md = mcs * 128
            wdst = wbuf if md == 1024 else wbuf2
            ctx.sp.wait(ctx.sy)
            wld.inc(ctx.sy.dma_start(
                out=wdst[:], in_=ws[wnm][:, :].rearrange("(kc p) m -> p kc m", p=128)), 16)
            wld.wait(ctx.te)
            ctx.st.wait(ctx.sc)
            ctx.st.wait(ctx.v)
            cbp = emit_copy_evac(ctx, outb, func=func, bias=bias)
            emit_proj(ctx, ps_mm, wdst, xn1b, list(range(8)), mcs, TC, cbp)
            ctx.out_act(name, md, TC, BF16, outb[:, 0:mcs, :] if mcs < 8 else outb)
        blog = ctx.sb("blog", [8, 1, TC], F32)
        if "blog" not in stages:
            ctx.finish()
            return ctx
        for t0 in range(0, TC, NT):
            ctx.ss.wait(ctx.te)
            ctx.sv.wait(ctx.te)
            for i in range(8):
                inst = ctx.te.matmul(ps_b[:, :], wbT[:, i, 0:8], xn1b[:, i, t0:t0 + NT],
                                     start=(i == 0), stop=(i == 7))
            ctx.sp.inc(inst)
            ctx.sp.wait(ctx.v)
            ctx.v.tensor_copy(blog[0:8, 0, t0:t0 + NT], ps_b[:, :])
            ctx.sv.inc(ctx.v.tensor_scalar(blog[0:8, 0, t0:t0 + NT],
                                           blog[0:8, 0, t0:t0 + NT],
                                           bb[0:8, 0:1], None, ALU.add))
        hbm = ctx.dram_out("blog", [8, TC], F32)
        ctx.ss.wait(ctx.nc.gpsimd)
        ctx.sv.wait(ctx.nc.gpsimd)
        ctx.st.inc(ctx.gp.dma_start(out=hbm[:, :], in_=blog[:, 0, :]), 16)
        ctx.finish()
    return ctx



def build_l1a2():
    ctx = Ctx("l1a2")
    nc = ctx.nc
    with nc.Block():
        xn1b = ctx.in_act("xn1b", D, TC, BF16)
        wfc = ctx.in_weight("W_fcT", D, 4096)
        wpr = ctx.in_weight("W_projT", 4096, D)
        yh = ctx.dram_out("y", [D, TC], F32)
        hfc = ctx.sb("hfc", [128, 32, NT], BF16)
        rsc = ctx.sb("rsc", [128, NT], BF16)
        ps_mm = ctx.psum("ps_mm", [128, NT])
        ps_y = ctx.psum("ps_y", [128, NT])
        ctx.ld.wait(ctx.te)
        ctx.ld.wait(ctx.sc)
        ctx.ld.wait(ctx.v)
        for tt in range(TC // NT):
            t0 = tt * NT
            for mc in range(32):
                ctx.ss.wait(ctx.te)
                ctx.sv.wait(ctx.te)
                for i in range(8):
                    inst = ctx.te.matmul(ps_mm[:, :], wfc[:, i, mc * 128:(mc + 1) * 128],
                                         xn1b[:, i, t0:t0 + NT], start=(i == 0), stop=(i == 7))
                ctx.sp.inc(inst)
                ctx.sp.wait(ctx.sc)
                ctx.ss.inc(ctx.sc.activation(rsc[:, :], ps_mm[:, :], AF.Relu))
                ctx.ss.wait(ctx.v)
                ctx.sv.inc(ctx.v.tensor_tensor(hfc[:, mc, :], rsc[:, :], rsc[:, :], ALU.mult))
            for oc_ in range(8):
                ctx.sv.wait(ctx.te)
                ctx.ss.wait(ctx.te)
                for i in range(32):
                    inst = ctx.te.matmul(ps_y[:, :], wpr[:, i, oc_ * 128:(oc_ + 1) * 128],
                                         hfc[:, i, :], start=(i == 0), stop=(i == 31))
                ctx.sp.inc(inst)
                ctx.sp.wait(ctx.sc)
                ctx.st.wait(ctx.sc)
                ctx.ss.inc(ctx.sc.activation(ctx.evac_scr()[:, :], ps_y[:, :], AF.Copy))
                ctx.ss.wait(ctx.gp)
                ctx.st.inc(ctx.gp.dma_start(
                    out=yh[oc_ * 128:(oc_ + 1) * 128, t0:t0 + NT],
                    in_=ctx.evac_scr()[:, :]), 16)
        ctx.finish()
    return ctx



def build_l1a3():
    ctx = Ctx("l1a3")
    nc = ctx.nc
    with nc.Block():
        y = ctx.in_act("y", D, TC, F32)
        m = ctx.sb("m", [128, 8, TC], F32)
        spk = ctx.sb("spk", [128, 8, TC], BF16)
        accs = ctx.sb("accs", [128, 8, TC], BF16)
        scr = ctx.sb("scr2", [128, TC], BF16)
        gth = ctx.dram_out("gtop", [D, TC], BF16)
        ctx.ld.wait(ctx.v)
        v = ctx.v
        for c in range(8):
            v.memset(m[:, c, :], 0.0)
            v.memset(spk[:, c, :], 0.0)
            ins = v.memset(accs[:, c, :], 0.0)
        ctx.sv.inc(ins)
        for step in range(5):
            for c in range(8):
                v.scalar_tensor_tensor(m[:, c, :], m[:, c, :], 0.9, y[:, c, :],
                                       ALU.mult, ALU.add)
                v.tensor_tensor(m[:, c, :], m[:, c, :], spk[:, c, :], ALU.subtract)
                v.tensor_single_scalar(spk[:, c, :], m[:, c, :], 1.0, ALU.is_gt)
                ins = v.tensor_tensor(accs[:, c, :], accs[:, c, :], spk[:, c, :], ALU.add)
        ctx.sv.inc(ins)
        ctx.sv.wait(ctx.sc)
        for c in range(8):
            ctx.st.wait(ctx.sc)
            ctx.ss.inc(ctx.sc.activation(scr[:, :], accs[:, c, :], AF.Silu, scale=0.2))
            ctx.ss.wait(ctx.gp)
            ctx.st.inc(ctx.gp.dma_start(out=gth[c * 128:(c + 1) * 128, :], in_=scr[:, :]), 16)
        ctx.finish()
    return ctx



CC = 32
LSEQ = 8192


def emit_conv_silu(ctx, out_bf, in_hbm_name, ch, w_sb, b_sb, scale, pad_bf, accf):
    v, sc = ctx.v, ctx.sc
    hbm = ctx.dram_in(in_hbm_name, [ch, LSEQ], BF16)
    ctx.sv.wait(ctx.sy)
    ctx.ld.inc(ctx.sy.dma_start(out=pad_bf[0:ch, 3:3 + LSEQ], in_=hbm[:, :]), 16)
    v.memset(pad_bf[0:ch, 0:3], 0.0)
    ctx.ld.wait(v)
    HALF = LSEQ // 2
    for h0 in (0, HALF):
        v.tensor_scalar(accf[0:ch, h0:h0 + HALF], pad_bf[0:ch, 3 + h0 - 3:3 + h0 - 3 + HALF],
                        w_sb[0:ch, 0:1], None, ALU.mult)
        for j in (1, 2, 3):
            v.scalar_tensor_tensor(accf[0:ch, h0:h0 + HALF],
                                   pad_bf[0:ch, j + h0:j + h0 + HALF],
                                   w_sb[0:ch, j:j + 1], accf[0:ch, h0:h0 + HALF],
                                   ALU.mult, ALU.add)
    ctx.sv.inc(v.nop())
    ctx.sv.wait(sc)
    ctx.ss.inc(sc.activation(out_bf[0:ch, 0:LSEQ], accf[0:ch, 0:LSEQ], AF.Silu,
                             bias=b_sb[0:ch, 0:1]))
    if scale != 1.0:
        ctx.ss.wait(v)
        ctx.sv.inc(v.tensor_scalar_mul(out_bf[0:ch, 0:LSEQ], out_bf[0:ch, 0:LSEQ], float(scale)))


def build_l1b():
    ctx = Ctx("l1b")
    nc = ctx.nc
    NCH = LSEQ // CC
    with nc.Block():
        oc, orow = ones_consts(ctx)
        ident = ctx.const_sb("ident", np.eye(128, dtype=BF), BF16)
        identf = ctx.const_sb("identf", np.eye(128, dtype=np.float32), F32)
        E = np.zeros((CC, CC * 128), dtype=BF)
        for t in range(CC):
            E[t, t * 128:(t + 1) * 128] = 1
        E_all = ctx.const_sb("E_all", E, BF16)
        LM = (np.tril(np.ones((CC, CC), np.float32)).T).copy()
        lmask = ctx.const_sb("lmask", LM, F32)
        zero32 = ctx.sb("zero32", [128, CC], F32)
        ctx.v.memset(zero32[:], 0.0)

        pad = ctx.sb("pad", [128, 3 + LSEQ], BF16)
        accf = ctx.sb("accf", [128, LSEQ], F32)
        qs = ctx.sb("qs", [64, LSEQ], BF16)
        ks = ctx.sb("ks", [64, LSEQ], BF16)
        vs = ctx.sb("vs", [128, LSEQ], BF16)
        alog = ctx.sb("alog", [128, LSEQ], BF16)
        betr = ctx.sb("betr", [1, LSEQ], F32)
        betb32 = ctx.sb("betb32", [1, CC], BF16)
        ob = ctx.sb("ob", [128, LSEQ], BF16)
        a_t = ctx.sb("a_t", [128, CC], F32)
        bbc = ctx.sb("bbc", [128, CC], F32)
        gqf = ctx.sb("gqf", [128, CC], F32)
        obf = ctx.sb("obf", [128, CC], F32)
        suf = ctx.sb("suf", [64, 128], F32)
        lamb2 = ctx.sb("lamb2", [64, 128], F32)
        lam = ctx.sb("lam", [128, CC], F32)
        rlam = ctx.sb("rlam", [128, CC], F32)
        ra = ctx.sb("ra", [128, CC], F32)
        dneg = ctx.sb("dneg", [128, CC], F32)
        cmat = ctx.sb("cmat", [128, CC], F32)
        ACCb = ctx.sb("ACCb", [128, CC], F32)
        W = ctx.sb("W", [128, CC], F32)
        Wbf = ctx.sb("Wbf", [128, CC], BF16)
        gbc = ctx.sb("gbc", [128, CC * CC], F32)
        gq = ctx.sb("gq", [CC, CC], BF16)
        wt = ctx.sb("wt", [CC, 128], BF16)
        kt = ctx.sb("kt", [CC, 64], BF16)
        S = ctx.sb("S", [64, 128], F32)
        Sbf = ctx.sb("Sbf", [64, 128], BF16)
        lrow = ctx.sb("lrow", [1, 128], BF16)
        ps_a = ctx.psum("ps_a", [128, NT])
        ps_g = ctx.psum("ps_g", [128, 2 * NT])
        ps_t = ctx.psum("ps_t", [128, 192], BF16)
        ps_o = ctx.psum("ps_o", [128, NT])
        ln_w = ctx.in_vec("ln_w", 128)
        ln_b = ctx.in_vec("ln_b", 128)

        for p in range(2):
            cws = {}
            for nm, chn in (("qc_w", 64), ("qc_b", 64), ("kc_w", 64), ("kc_b", 64),
                            ("vc_w", 128), ("vc_b", 128)):
                cols = 4 if nm.endswith("w") else 1
                hbm = ctx.dram_in(f"{nm}{p}", [chn, cols], F32)
                t_ = ctx.sb(f"{nm}{p}_sb", [chn, cols], F32)
                ctx.ld.inc(ctx.sy.dma_start(out=t_[:], in_=hbm[:, :]), 16)
                cws[nm] = t_
            ctx.ld.wait(ctx.v)
            ctx.ld.wait(ctx.sc)
            ctx.st.wait(ctx.v)
            emit_conv_silu(ctx, qs, f"qpre{p}", 64, cws["qc_w"], cws["qc_b"], 1.0, pad, accf)
            emit_conv_silu(ctx, ks, f"kpre{p}", 64, cws["kc_w"], cws["kc_b"], 64 ** -0.5, pad, accf)
            emit_conv_silu(ctx, vs, f"vpre{p}", 128, cws["vc_w"], cws["vc_b"], 1.0, pad, accf)
            ahbm = ctx.dram_in(f"alog{p}", [128, LSEQ], BF16)
            ctx.sv.wait(ctx.sy)
            ctx.ss.wait(ctx.sy)
            ctx.ld.inc(ctx.sy.dma_start(out=alog[:, :], in_=ahbm[:, :]), 16)
            bhbm = ctx.dram_in(f"blog{p}", [1, LSEQ], F32)
            ctx.ld.inc(ctx.sy.dma_start(out=betr[:, :], in_=bhbm[:, :]), 16)
            ctx.ld.wait(ctx.sc)
            ctx.ss.inc(ctx.sc.activation(betr[:, :], betr[:, :], AF.Sigmoid))
            ctx.v.memset(S[:, :], 0.0)
            ctx.sv.inc(ctx.v.memset(Sbf[:, :], 0.0))

            for c in range(NCH):
                sl = slice(c * CC, (c + 1) * CC)
                v, sc, te = ctx.v, ctx.sc, ctx.te
                ctx.ss.wait(te)
                ctx.sv.wait(te)
                inst = te.matmul(ps_o[0:CC, 0:CC], ks[:, sl], ks[:, sl], start=True, stop=True)
                ctx.sp.inc(inst)
                ctx.sp.wait(sc)
                ctx.ss.inc(sc.activation(gq[:, :], ps_o[0:CC, 0:CC], AF.Copy))
                ctx.ss.wait(te)
                for t in range(CC):
                    inst = te.matmul(ps_g[:, t * CC:(t + 1) * CC],
                                     E_all[0:CC, t * 128:(t + 1) * 128], gq[:, :],
                                     start=True, stop=True)
                ctx.sp.inc(inst)
                ctx.sp.wait(sc)
                sc.activation(gbc[:, 0:NT], ps_g[:, 0:NT], AF.Copy)
                ctx.ss.inc(sc.activation(gbc[:, NT:2 * NT], ps_g[:, NT:2 * NT], AF.Copy))
                ctx.sv.wait(te)
                ctx.ss.wait(te)
                ctx.ss.wait(ctx.v)
                ctx.v.tensor_copy(betb32[0:1, :], betr[0:1, sl])
                ctx.sv.wait(te)
                inst = te.matmul(ps_a[:, 0:CC], orow[0:1, :], betb32[0:1, :],
                                 start=True, stop=True)
                ctx.sp.inc(inst)
                ctx.sv.wait(sc)
                ctx.sp.wait(sc)
                sc.activation(bbc[:, :], ps_a[:, 0:CC], AF.Copy)
                ctx.ss.inc(sc.activation(a_t[:, :], alog[:, sl], AF.Sigmoid))
                ctx.ss.wait(v)
                v.tensor_tensor_scan(lam[:, :], a_t[:, :], zero32[:, :], 1.0, ALU.mult, ALU.add)
                v.reciprocal(rlam[:, :], lam[:, :])
                v.reciprocal(ra[:, :], a_t[:, :])
                v.tensor_tensor(dneg[:, :], ra[:, :], bbc[:, :], ALU.mult)
                v.tensor_scalar_mul(dneg[:, :], dneg[:, :], -1.0)
                v.tensor_tensor(cmat[:, :], vs[:, sl], bbc[:, :], ALU.mult)
                ctx.sv.inc(v.tensor_tensor(cmat[:, :], cmat[:, :], rlam[:, :], ALU.mult))
                ctx.sv.wait(te)
                ctx.ss.wait(te)
                inst = te.matmul(ps_a[:, 256:256 + CC], Sbf[:, :], ks[:, sl],
                                 start=True, stop=True)
                ctx.sp.inc(inst)
                ctx.sp.wait(v)
                ctx.sv.inc(v.tensor_copy(ACCb[:, :], ps_a[:, 256:256 + CC]))
                for t in range(CC):
                    v.tensor_scalar(W[:, t:t + 1], ACCb[:, t:t + 1], dneg[:, t:t + 1],
                                    cmat[:, t:t + 1], ALU.mult, ALU.add)
                    if t < CC - 1:
                        v.scalar_tensor_tensor(ACCb[:, t + 1:CC],
                                               gbc[:, t * CC + t + 1:t * CC + CC],
                                               W[:, t:t + 1], ACCb[:, t + 1:CC],
                                               ALU.mult, ALU.add)
                ctx.sv.inc(v.tensor_copy(Wbf[:, :], W[:, :]))
                ctx.sv.wait(te)
                ctx.ss.wait(te)
                inst = te.transpose(ps_t[0:CC, 0:128], Wbf[:, :], ident[:, :])
                ctx.sp.inc(inst)
                ctx.sp.wait(sc)
                ctx.ss.inc(sc.activation(wt[:, :], ps_t[0:CC, 0:128], AF.Copy))
                ctx.ss.wait(te)
                inst = te.transpose(ps_t[0:CC, 128:192], ks[:, sl], ident[0:64, 0:64])
                ctx.sp.inc(inst)
                ctx.sp.wait(sc)
                ctx.ss.inc(sc.activation(kt[:, :], ps_t[0:CC, 128:192], AF.Copy))
                ctx.ss.wait(te)
                ctx.sv.wait(te)
                inst = te.matmul(ps_o[0:CC, 256:256 + CC], ks[:, sl], qs[:, sl],
                                 start=True, stop=True)
                ctx.sp.inc(inst)
                ctx.sp.wait(v)
                v.tensor_copy(gqf[0:CC, 0:CC], ps_o[0:CC, 256:256 + CC])
                ctx.sv.inc(v.tensor_tensor(gq[:, :], gqf[0:CC, 0:CC],
                                           lmask[:, :], ALU.mult))
                ctx.sv.wait(te)
                te.matmul(ps_o[:, 0:CC], wt[:, :], gq[:, :], start=True, stop=False)
                inst = te.matmul(ps_o[:, 0:CC], Sbf[:, :], qs[:, sl], start=False, stop=True)
                ctx.sp.inc(inst)
                ctx.sp.wait(v)
                v.tensor_copy(obf[:, :], ps_o[:, 0:CC])
                ctx.sv.inc(v.tensor_tensor(ob[:, sl], obf[:, :], lam[:, :], ALU.mult))
                ctx.ss.wait(te)
                ctx.sv.wait(te)
                inst = te.matmul(ps_o[0:64, 256:256 + 128], kt[:, :], wt[:, :],
                                 start=True, stop=True)
                ctx.sp.inc(inst)
                inst = te.transpose(ps_a[0:1, NT - 128:NT], lam[:, CC - 1:CC], identf[:, :])
                ctx.sp.inc(inst)
                ctx.sp.wait(sc)
                ctx.ss.inc(sc.activation(lrow[0:1, 0:128], ps_a[0:1, NT - 128:NT], AF.Copy))
                ctx.ss.wait(te)
                ctx.sv.wait(te)
                inst = te.matmul(ps_a[0:64, 128:256], orow[0:1, 0:64], lrow[0:1, 0:128],
                                 start=True, stop=True)
                ctx.sp.inc(inst)
                ctx.sp.wait(v)
                v.tensor_copy(suf[:, :], ps_o[0:64, 256:256 + 128])
                v.tensor_copy(lamb2[:, :], ps_a[0:64, 128:256])
                v.tensor_tensor(S[:, :], S[:, :], suf[:, :], ALU.add)
                v.tensor_tensor(S[:, :], S[:, :], lamb2[:, :], ALU.mult)
                ctx.sv.inc(v.tensor_copy(Sbf[:, :], S[:, :]))
            emit_groupnorm_128(ctx, ob, ln_w, ln_b, oc, orow, ps_g, sfx=str(p))
            hbm_o = ctx.dram_out(f"o{p}", [128, LSEQ], BF16)
            ctx.sv.wait(ctx.gp)
            ctx.ss.wait(ctx.gp)
            ctx.st.inc(ctx.gp.dma_start(out=hbm_o[:, :], in_=ob[:, :]), 16)
        ctx.finish()
    return ctx


def emit_groupnorm_128(ctx, ob, ln_w, ln_b, oc, orow, ps2, sfx=""):
    nc = ctx.nc
    v, sc, te = ctx.v, ctx.sc, ctx.te
    if not hasattr(ctx, "_gn_scr"):
        ctx._gn_scr = dict(
            rows=ctx.sb("gnS_rows", [1, 4 * NT], F32),
            rows_bf=ctx.sb("gnS_rows_bf", [1, 2 * NT], BF16),
            sq=ctx.sb("gnS_sq", [128, NT], BF16),
            scr=ctx.sb("gnS_scr", [128, NT], F32),
            bcv=ctx.sb("gnS_bcv", [128, 2 * NT], F32),
        )
    G = ctx._gn_scr
    rows, rows_bf, sq, scr = G["rows"], G["rows_bf"], G["sq"], G["scr"]
    bcv = G["bcv"]
    for t0 in range(0, LSEQ, NT):
        n = NT
        mu = rows[0:1, 0:n]
        msq = rows[0:1, NT:NT + n]
        ctx.sp.wait(v)
        ctx.sv.inc(v.tensor_tensor(sq[:, :], ob[:, t0:t0 + n], ob[:, t0:t0 + n], ALU.mult))
        ctx.sv.wait(te)
        ctx.ss.wait(te)
        te.matmul(ps2[0:1, 0:n], oc[:, 0:1], ob[:, t0:t0 + n], start=True, stop=True)
        inst = te.matmul(ps2[0:1, NT:NT + n], oc[:, 0:1], sq[:, :], start=True, stop=True)
        ctx.sp.inc(inst)
        ctx.sp.wait(sc)
        sc.activation(mu, ps2[0:1, 0:n], AF.Copy, scale=1.0 / 128)
        ctx.ss.inc(sc.activation(msq, ps2[0:1, NT:NT + n], AF.Copy, scale=1.0 / 128))
        A = rows[0:1, 2 * NT:2 * NT + n]
        B = rows[0:1, 3 * NT:3 * NT + n]
        ctx.ss.wait(v)
        v.tensor_tensor(A, mu, mu, ALU.mult)
        ctx.sv.inc(v.tensor_sub(A, msq, A))
        ctx.sv.wait(sc)
        sc.activation(A, A, AF.Ln, bias=ctx.cap(1e-5, rows=1))
        ctx.ss.inc(sc.activation(A, A, AF.Exp, scale=-0.5))
        ctx.ss.wait(v)
        v.tensor_tensor(B, mu, A, ALU.mult)
        v.tensor_scalar_mul(B, B, -1.0)
        v.tensor_copy(rows_bf[0:1, 0:n], A)
        ctx.sv.inc(v.tensor_copy(rows_bf[0:1, NT:NT + n], B))
        ctx.sv.wait(te)
        te.matmul(ps2[:, 0:n], orow[0:1, :], rows_bf[0:1, 0:n], start=True, stop=True)
        inst = te.matmul(ps2[:, NT:NT + n], orow[0:1, :], rows_bf[0:1, NT:NT + n],
                         start=True, stop=True)
        ctx.sp.inc(inst)
        ctx.sp.wait(v)
        v.tensor_copy(bcv[:, 0:n], ps2[:, 0:n])
        v.tensor_copy(bcv[:, NT:NT + n], ps2[:, NT:NT + n])
        v.tensor_tensor(scr[:, :], ob[:, t0:t0 + n], bcv[:, 0:n], ALU.mult)
        ctx.sv.inc(v.tensor_tensor(scr[:, :], scr[:, :], bcv[:, NT:NT + n], ALU.add))
        ctx.sv.inc(v.tensor_scalar(ob[:, t0:t0 + n], scr[:, :],
                                   ln_w[:, 0:1], ln_b[:, 0:1], ALU.mult, ALU.add))



def build_l1c():
    ctx = Ctx("l1c")
    nc = ctx.nc
    with nc.Block():
        o = ctx.in_act("o", D, TC, BF16)
        gate = ctx.in_act("gate", D, TC, BF16)
        gtop = ctx.in_act("gtop", D, TC, BF16)
        wo = ctx.in_weight("WoT", D, D)
        x1 = ctx.in_act("x1", D, TC, F32)
        x2h = ctx.dram_out("x2", [D, TC], F32)
        og = ctx.sb("og", [128, 8, NT], BF16)
        x2t = ctx.sb("x2t", [128, 8, NT], F32)
        scr = ctx.sb("scr", [128, NT], F32)
        ps_mm = ctx.psum("ps_mm", [128, NT])
        ctx.ld.wait(ctx.v)
        ctx.ld.wait(ctx.te)
        ctx.ld.wait(ctx.sc)
        for tt in range(TC // NT):
            t0 = tt * NT
            ctx.sp.wait(ctx.v)
            for c in range(8):
                ins = ctx.v.tensor_tensor(og[:, c, :], o[:, c, t0:t0 + NT],
                                          gate[:, c, t0:t0 + NT], ALU.mult)
            ctx.sv.inc(ins)
            ctx.st.wait(ctx.v)
            for mc in range(8):
                ctx.sv.wait(ctx.te)
                for i in range(8):
                    inst = ctx.te.matmul(ps_mm[:, :], wo[:, i, mc * 128:(mc + 1) * 128],
                                         og[:, i, :], start=(i == 0), stop=(i == 7))
                ctx.sp.inc(inst)
                ctx.sp.wait(ctx.v)
                ctx.v.tensor_copy(scr[:, :], ps_mm[:, :])
                ctx.v.tensor_tensor(scr[:, :], scr[:, :], gtop[:, mc, t0:t0 + NT], ALU.mult)
                ctx.sv.inc(ctx.v.tensor_tensor(x2t[:, mc, :], scr[:, :],
                                               x1[:, mc, t0:t0 + NT], ALU.add))
            ctx.sv.wait(ctx.gp)
            ctx.st.inc(ctx.gp.dma_start(
                out=x2h[:, t0:t0 + NT].rearrange("(dc p) t -> p dc t", p=128),
                in_=x2t[:]), 16)
        ctx.finish()
    return ctx



def _run(ctx, in_maps, **kw):
    res = run_bass_kernel_spmd(ctx.nc, in_maps, core_ids=list(range(NCORES)), **kw)
    return res.results


def _t(a):
    return np.ascontiguousarray(np.asarray(a).T)


def host_prep(inp):
    W = {}
    cast_specs = []

    def add_cast(name, arr):
        cast_specs.append((name, np.ascontiguousarray(np.asarray(arr, np.float32))))

    add_cast("l0_wqT", _t(inp["L0_attn_wq"]))
    add_cast("l0_wkT", _t(inp["L0_attn_wk"]))
    add_cast("l0_wvT", _t(inp["L0_attn_wv"]))
    add_cast("l0_w1T", _t(inp["L0_attn_w1"]))
    add_cast("l0_w2T", _t(inp["L0_attn_w2"]))
    add_cast("l0_Win", inp["L0_ffn_W_in"])
    add_cast("l0_Wgate", inp["L0_ffn_W_gate"])
    add_cast("l0_W1", np.asarray(inp["L0_ffn_W1"]).reshape(8 * 1024, 4096))
    add_cast("l0_W2", np.asarray(inp["L0_ffn_W2"]).reshape(8 * 4096, 1024))
    add_cast("l0_Wout", inp["L0_ffn_W_out"])
    add_cast("l0_b2e", inp["L0_ffn_b2"])
    add_cast("l1_WqT", _t(inp["L1_attn_Wq"]))
    add_cast("l1_WkT", _t(inp["L1_attn_Wk"]))
    add_cast("l1_WvT", _t(inp["L1_attn_Wv"]))
    add_cast("l1_WaT", _t(inp["L1_attn_Wa"]))
    add_cast("l1_WbT", _t(inp["L1_attn_Wb"]))
    add_cast("l1_WgT", _t(inp["L1_attn_Wg"]))
    add_cast("l1_WoT", _t(inp["L1_attn_Wo"]))
    add_cast("l1_WfcT", _t(inp["L1_ffn_W_fc"]))
    add_cast("l1_WprT", _t(inp["L1_ffn_W_proj"]))
    perm = np.concatenate([np.arange(0, D, 2), np.arange(1, D, 2)])
    add_cast("l2_wqT", _t(np.asarray(inp["L2_attn_wq"])[perm, :]))
    add_cast("l2_wkT", _t(np.asarray(inp["L2_attn_wk"])[perm, :]))
    add_cast("l2_wvT", _t(np.asarray(inp["L2_attn_wv"])[perm, :]))
    ipw = np.asarray(inp["L2_attn_in_proj_w"])
    add_cast("l2_wqiT", _t(ipw[0:D][:, perm]))
    add_cast("l2_wkiT", _t(ipw[D:2 * D][:, perm]))
    add_cast("l2_wviT", _t(ipw[2 * D:3 * D][:, perm]))
    add_cast("l2_woT", _t(inp["L2_attn_out_w"]))
    add_cast("l3_WinT", _t(inp["L3_attn_W_in"]))
    xp = np.asarray(inp["L3_attn_W_xproj"])
    add_cast("l3_WxdT", _t(xp[0:64]))
    add_cast("l3_WxbT", _t(xp[64:80]))
    add_cast("l3_WxcT", _t(xp[80:96]))
    add_cast("l3_WdtT", _t(inp["L3_attn_W_dt"]))
    add_cast("l3_WoutT", _t(inp["L3_attn_W_out"]))

    sizes = [(n, a.shape, a.size) for n, a in cast_specs]
    flat = np.concatenate([a.ravel() for _, a in cast_specs])
    percore = -(-flat.size // (NCORES * 128))
    percore = -(-percore // 512) * 512
    pad = np.zeros(NCORES * 128 * percore - flat.size, np.float32)
    flat3 = np.concatenate([flat, pad]).reshape(NCORES, 128, percore)
    qnames = ["L3_ffn_w_gate", "L3_ffn_w_up", "L3_ffn_w_down"]
    qarrs = [np.ascontiguousarray(np.asarray(inp[n], np.float32)) for n in qnames]
    qshapes = [a.shape for a in qarrs]
    ctx = build_prep(percore, qshapes)
    in_maps = []
    for c in range(NCORES):
        m = {"cast_in": flat3[c]}
        for i, a in enumerate(qarrs):
            m[f"q_in{i}"] = a
        in_maps.append(m)
    res = _run(ctx, in_maps)
    flat_bf = np.concatenate([res[c]["cast_out"].ravel() for c in range(NCORES)])
    off = 0
    for n, shp, sz in sizes:
        W[n] = flat_bf[off:off + sz].reshape(shp)
        off += sz
    W["l3_wgq"] = _t(res[0]["q_out0"])
    W["l3_wuq"] = _t(res[0]["q_out1"])
    W["l3_wdq"] = _t(res[0]["q_out2"])

    W["l0_g1"] = vec_fm(inp["L0_attn_g1"])
    W["l0_be1"] = vec_fm(inp["L0_attn_be1"])
    W["l0_g2"] = vec_fm(inp["L0_attn_g2"])
    W["l0_be2"] = vec_fm(inp["L0_attn_be2"])
    W["l0_b1"] = vec_fm(inp["L0_attn_b1"])
    W["l0_b2"] = vec_fm(inp["L0_attn_b2"])
    W["l0_bin"] = vec_fm(inp["L0_ffn_b_in"])
    W["l0_bgate"] = np.ascontiguousarray(np.tile(np.asarray(inp["L0_ffn_b_gate"], np.float32)[None, :], (128, 1)))
    W["l0_b1e"] = np.concatenate([vec_fm(np.asarray(inp["L0_ffn_b1"])[e]) for e in range(8)], axis=1)
    W["l0_bout"] = vec_fm(inp["L0_ffn_b_out"])
    W["l1_ba"] = vec_fm(inp["L1_attn_ba"])
    W["l1_bb"] = np.ascontiguousarray(np.asarray(inp["L1_attn_bb"], np.float32).reshape(8, 1))
    W["l1_lnw"] = np.ascontiguousarray(np.asarray(inp["L1_attn_ln_w"], np.float32).reshape(128, 1))
    W["l1_lnb"] = np.ascontiguousarray(np.asarray(inp["L1_attn_ln_b"], np.float32).reshape(128, 1))
    for nm, key in (("qc", "L1_attn_qc_w"), ("kc", "L1_attn_kc_w"), ("vc", "L1_attn_vc_w")):
        W[f"l1_{nm}w"] = np.ascontiguousarray(np.asarray(inp[key], np.float32)[:, 0, :])
    W["l1_qcb"] = np.ascontiguousarray(np.asarray(inp["L1_attn_qc_b"], np.float32).reshape(-1, 1))
    W["l1_kcb"] = np.ascontiguousarray(np.asarray(inp["L1_attn_kc_b"], np.float32).reshape(-1, 1))
    W["l1_vcb"] = np.ascontiguousarray(np.asarray(inp["L1_attn_vc_b"], np.float32).reshape(-1, 1))
    ipb = np.asarray(inp["L2_attn_in_proj_b"], np.float32)
    W["l2_bqi"] = vec_fm(ipb[0:D])
    W["l2_bki"] = vec_fm(ipb[D:2 * D])
    W["l2_bvi"] = vec_fm(ipb[2 * D:3 * D])
    W["l2_bvi_row"] = np.ascontiguousarray(ipb[2 * D:3 * D].astype(np.float32).reshape(1, D))
    W["l2_bo"] = vec_fm(inp["L2_attn_out_b"])
    W["l2_lng"] = vec_fm(inp["L2_ffn_ln_g"])
    W["l2_lnb"] = vec_fm(inp["L2_ffn_ln_b"])
    W["l3_convw"] = np.ascontiguousarray(np.asarray(inp["L3_attn_conv_w"], np.float32)[:, 0, :])
    W["l3_convb"] = vec_fm(inp["L3_attn_conv_b"])
    W["l3_bdt"] = vec_fm(inp["L3_attn_b_dt"])
    W["l3_Alog"] = np.ascontiguousarray(np.tile(
        np.asarray(inp["L3_attn_A_log"], np.float32).reshape(1, 16), (128, 1)))
    W["l3_D"] = np.ascontiguousarray(np.asarray(inp["L3_attn_D"], np.float32))
    return W


def _slices(full, dt=None):
    return [np.ascontiguousarray(full[:, c * TC:(c + 1) * TC]) for c in range(NCORES)]


def _gather(res, key):
    return np.concatenate([res[c][key] for c in range(NCORES)], axis=1)


def run_layer0(X, W, keep=None):
    keep = keep if keep is not None else {}
    ctx = build_l0a()
    xs = _slices(X)
    im = [dict(x=xs[c], wqT=W["l0_wqT"], wkT=W["l0_wkT"], wvT=W["l0_wvT"],
               g1=W["l0_g1"], be1=W["l0_be1"]) for c in range(NCORES)]
    r = _run(ctx, im)
    xn = _gather(r, "xn")
    Q = _gather(r, "Q")
    K = _gather(r, "K")
    V = _gather(r, "V")
    keep.update(xn=xn, Q=Q, K=K, V=V)

    ctx = build_l0b()
    im = []
    for c in range(NCORES):
        m = {}
        for p in range(2):
            pr = 2 * c + p
            b, h = pr // 8, pr % 8
            m[f"q{p}"] = np.ascontiguousarray(Q[h * 128:(h + 1) * 128, b * 8192:(b + 1) * 8192])
            m[f"kt{p}"] = _t(K[h * 128:(h + 1) * 128, b * 8192:(b + 1) * 8192])
            m[f"vt{p}"] = _t(V[h * 128:(h + 1) * 128, b * 8192:(b + 1) * 8192])
        im.append(m)
    r = _run(ctx, im)
    msg = np.zeros((D, TTOT), BF)
    for c in range(NCORES):
        for p in range(2):
            pr = 2 * c + p
            b, h = pr // 8, pr % 8
            msg[h * 128:(h + 1) * 128, b * 8192:(b + 1) * 8192] = r[c][f"msg{p}"]
    keep["msg"] = msg

    ctx = build_l0c1()
    xns = _slices(xn)
    msgs = _slices(msg)
    im = [dict(xn=xns[c], msg=msgs[c], g2=W["l0_g2"], be2=W["l0_be2"]) for c in range(NCORES)]
    r = _run(ctx, im)
    x0 = _gather(r, "x0")
    x3b = _gather(r, "x3b")
    keep.update(x0=x0, x3b=x3b)

    ctx = build_l0c2()
    x3s = _slices(x3b)
    im = [dict(x3b=x3s[c], w1T=W["l0_w1T"], b1=W["l0_b1"]) for c in range(NCORES)]
    r = _run(ctx, im)
    h = _gather(r, "h")

    ctx = build_l0c3()
    hs = _slices(h)
    x0s = _slices(x0)
    im = [dict(h=hs[c], w2T=W["l0_w2T"], b2=W["l0_b2"], x0=x0s[c]) for c in range(NCORES)]
    r = _run(ctx, im)
    s = _gather(r, "s")
    keep["s"] = s

    ctx = build_l0d1()
    im = [dict(xn=xns[c], W_in=W["l0_Win"], b_in=W["l0_bin"], W_gate=W["l0_Wgate"],
               b_gate=W["l0_bgate"]) for c in range(NCORES)]
    r = _run(ctx, im)
    t_full = _gather(r, "t")
    route_tok = [r[c]["route_tok"] for c in range(NCORES)]
    keep["route"] = np.concatenate(route_tok, axis=0)

    ctx = build_l0d2()
    ts = _slices(t_full)
    im = [dict(t=ts[c], W1=W["l0_W1"], W2=W["l0_W2"], b1e=W["l0_b1e"],
               route_tok=route_tok[c], route_fm=_t(route_tok[c]), b2e=W["l0_b2e"])
          for c in range(NCORES)]
    r = _run(ctx, im)
    moe = np.concatenate([r[c]["moe_tok"] for c in range(NCORES)], axis=0)
    keep["moe"] = moe

    ctx = build_l0d3()
    moeT = _t(moe)
    moeTs = _slices(moeT)
    ss = _slices(s)
    im = [dict(moeT=moeTs[c], W_out=W["l0_Wout"], b_out=W["l0_bout"], x=xs[c], s=ss[c])
          for c in range(NCORES)]
    r = _run(ctx, im)
    x1 = _gather(r, "x1")
    return np.ascontiguousarray(x1.astype(np.float32)), keep


def run_layer1(X1, W, keep=None):
    keep = keep if keep is not None else {}
    x1s = _slices(X1)
    ctx = build_l1a()
    im = [dict(x1=x1s[c], WqT=W["l1_WqT"], WkT=W["l1_WkT"], WvT=W["l1_WvT"],
               WaT=W["l1_WaT"], WgT=W["l1_WgT"], WbT=W["l1_WbT"],
               ba=W["l1_ba"], bb=W["l1_bb"]) for c in range(NCORES)]
    r = _run(ctx, im)
    xn1b = _gather(r, "xn1b")
    qpre = _gather(r, "qpre")
    kpre = _gather(r, "kpre")
    vpre = _gather(r, "vpre")
    alog = _gather(r, "alog")
    gate = _gather(r, "gate")
    blog = _gather(r, "blog")
    keep.update(xn1b=xn1b, qpre=qpre, kpre=kpre, vpre=vpre, alog=alog, gate=gate, blog=blog)

    ctx = build_l1a2()
    xnbs = _slices(xn1b)
    im = [dict(xn1b=xnbs[c], W_fcT=W["l1_WfcT"], W_projT=W["l1_WprT"]) for c in range(NCORES)]
    r = _run(ctx, im)
    y = _gather(r, "y")
    keep["y_snn"] = y

    ctx = build_l1a3()
    ys = _slices(y)
    im = [dict(y=ys[c]) for c in range(NCORES)]
    r = _run(ctx, im)
    gtop = _gather(r, "gtop")
    keep["gtop"] = gtop

    ctx = build_l1b()
    im = []
    for c in range(NCORES):
        m = dict(ln_w=W["l1_lnw"], ln_b=W["l1_lnb"])
        for p in range(2):
            pr = 2 * c + p
            b, h = pr // 8, pr % 8
            sl = slice(b * 8192, (b + 1) * 8192)
            m[f"qpre{p}"] = np.ascontiguousarray(qpre[h * 64:(h + 1) * 64, sl])
            m[f"kpre{p}"] = np.ascontiguousarray(kpre[h * 64:(h + 1) * 64, sl])
            m[f"vpre{p}"] = np.ascontiguousarray(vpre[h * 128:(h + 1) * 128, sl])
            m[f"alog{p}"] = np.ascontiguousarray(alog[h * 128:(h + 1) * 128, sl])
            m[f"blog{p}"] = np.ascontiguousarray(blog[h:h + 1, sl].astype(np.float32))
            m[f"qc_w{p}"] = np.ascontiguousarray(W["l1_qcw"][h * 64:(h + 1) * 64])
            m[f"qc_b{p}"] = np.ascontiguousarray(W["l1_qcb"][h * 64:(h + 1) * 64])
            m[f"kc_w{p}"] = np.ascontiguousarray(W["l1_kcw"][h * 64:(h + 1) * 64])
            m[f"kc_b{p}"] = np.ascontiguousarray(W["l1_kcb"][h * 64:(h + 1) * 64])
            m[f"vc_w{p}"] = np.ascontiguousarray(W["l1_vcw"][h * 128:(h + 1) * 128])
            m[f"vc_b{p}"] = np.ascontiguousarray(W["l1_vcb"][h * 128:(h + 1) * 128])
        im.append(m)
    r = _run(ctx, im)
    o = np.zeros((D, TTOT), BF)
    for c in range(NCORES):
        for p in range(2):
            pr = 2 * c + p
            b, h = pr // 8, pr % 8
            o[h * 128:(h + 1) * 128, b * 8192:(b + 1) * 8192] = r[c][f"o{p}"]
    keep["o_l1"] = o

    ctx = build_l1c()
    os_ = _slices(o)
    gs = _slices(gate)
    gts = _slices(keep["gtop"])
    im = [dict(o=os_[c], gate=gs[c], gtop=gts[c], WoT=W["l1_WoT"], x1=x1s[c])
          for c in range(NCORES)]
    r = _run(ctx, im)
    x2 = _gather(r, "x2")
    return np.ascontiguousarray(x2.astype(np.float32)), keep



def build_l2a(stages=("ln", "tables", "q", "k", "v")):
    ctx = Ctx("l2a")
    nc = ctx.nc
    with nc.Block():
        oc, orow = ones_consts(ctx)
        x2h = ctx.dram_in("x2", [D, TC], F32)
        x2t = ctx.sb("x2t", [128, 8, NT], F32)
        posd = ctx.dram_in("pos", [128, TC], F32)
        lng = ctx.in_vec("ln_g", D)
        lnb = ctx.in_vec("ln_b", D)
        i = np.arange(512, dtype=np.float64)
        th = 10000.0 ** (-2.0 * (i - 1.0) / 1024)
        thfm = np.ascontiguousarray(th.reshape(4, 128).T.astype(np.float32))
        theta = ctx.const_sb("theta", thfm, F32)
        pos = ctx.sb("pos_s", [128, TC], F32)
        ctx.ld.inc(ctx.sy.dma_start(out=pos[:], in_=posd[:, :]), 16)
        xn2b = ctx.sb("xn2b", [128, 8, TC], BF16)
        sint = ctx.sb("sint", [128, 4, TC], BF16)
        cost = ctx.sb("cost", [128, 4, TC], BF16)
        ang = ctx.sb("ang", [128, NT], F32)
        an2 = ctx.sb("an2", [128, NT], F32)
        ani = ctx.sb("ani", [128, NT], I32)
        wbuf = ctx.sb("wbuf", [128, 8, 1024], BF16)
        wibuf = ctx.sb("wibuf", [128, 8, 1024], BF16)
        rot = ctx.sb("rot", [128, 8, NT], BF16)
        eo = ctx.sb("eo", [128, 8, NT], F32)
        otile = ctx.sb("otile", [128, 8, NT], BF16)
        gt = ctx.sb("gt", [128, NT], BF16)
        r2bc = ctx.sb("r2bc", [128, NT], F32)
        gth = ctx.dram_out("g2top", [D, TC], BF16)
        ps_small = ctx.psum("ps_small", [1, NT])
        ps_bc = ctx.psum("ps_bc", [128, 2 * NT])
        ps_mm = ctx.psum("ps_mm", [128, NT])
        ctx.ld.wait(ctx.v)
        ctx.ld.wait(ctx.sc)
        ctx.ld.wait(ctx.te)
        v, sc, te = ctx.v, ctx.sc, ctx.te

        xld = SemC(nc, "xld")

        class TileView:
            def __getitem__(self, idx):
                _, c, tsl = idx
                return x2t[:, c, tsl.start % NT:(tsl.start % NT) + (tsl.stop - tsl.start)]

        def loader(t0, n):
            ctx.sv.wait(ctx.sy)
            ctx.sp.wait(ctx.sy)
            xld.inc(ctx.sy.dma_start(
                out=x2t[:, :, 0:n],
                in_=x2h[:, t0:t0 + n].rearrange("(dc p) t -> p dc t", p=128)), 16)
            xld.wait(ctx.nc.vector)
            return TileView()

        def cb(c, t0, n, xs):
            v.tensor_copy(xn2b[:, c, t0:t0 + n], xs)
            if c == 0:
                bcs = ctx._ln_scr["bcs"]
                v.tensor_tensor(r2bc[:, 0:n], bcs[:, 0:n], bcs[:, 0:n], ALU.mult)
                ctx.sv.inc(v.tensor_scalar_mul(r2bc[:, 0:n], r2bc[:, 0:n], -1e-5))
                ctx.sv.wait(sc)
                sc.activation(r2bc[:, 0:n], r2bc[:, 0:n], AF.Ln, bias=ctx.cap(1.0 + 1e-5))
                ctx.ss.inc(sc.activation(r2bc[:, 0:n], r2bc[:, 0:n], AF.Exp, scale=-0.5))
                ctx.ss.wait(v)
            ctx.st.wait(v)
            v.tensor_tensor(eo[:, 0, 0:n], xs, r2bc[:, 0:n], ALU.mult)
            v.tensor_scalar(eo[:, 0, 0:n], eo[:, 0, 0:n], lng[:, c:c + 1], lnb[:, c:c + 1],
                            ALU.mult, ALU.add)
            ctx.sv.wait(sc)
            ctx.ss.inc(sc.activation(gt[:, 0:n], eo[:, 0, 0:n], AF.Silu))
            ctx.ss.wait(ctx.nc.gpsimd)
            ctx.st.inc(ctx.gp.dma_start(out=gth[c * 128:(c + 1) * 128, t0:t0 + n],
                                        in_=gt[:, 0:n]), 16)
        if "ln" in stages:
            emit_ln(ctx, TC, 8, None, cb, oc=oc, orow=orow, ps_small=ps_small,
                    ps_bc=ps_bc, in_loader=loader)
        else:
            ctx.ld.wait(ctx.v)
            for c in range(8):
                for t0 in range(0, TC, NT):
                    ctx.sv.inc(ctx.v.tensor_copy(xn2b[:, c, t0:t0 + NT],
                                                 x2t[:, c, 0:NT]))
        ctx.out_act("xn2b", D, TC, BF16, xn2b)

        for c in (range(4) if "tables" in stages else []):
            for base, tdst in ((0.0, sint), (float(np.pi / 2), cost)):
                for t0 in range(0, TC, NT):
                    v.tensor_scalar(ang[:, :], pos[:, t0:t0 + NT], theta[:, c:c + 1],
                                    base, ALU.mult, ALU.add)
                    v.tensor_scalar_mul(an2[:, :], ang[:, :], float(1 / (2 * np.pi)))
                    v.tensor_copy(ani[:, :], an2[:, :])
                    v.tensor_copy(an2[:, :], ani[:, :])
                    v.scalar_tensor_tensor(an2[:, :], an2[:, :], float(-2 * np.pi),
                                           ang[:, :], ALU.mult, ALU.add)
                    ctx.sv.wait(sc)
                    ctx.ss.inc(sc.activation(tdst[:, c, t0:t0 + NT], an2[:, :], AF.Sin))

        wload = SemC(nc, "wload")
        for stage, wn, win, outname, tokmajor in (
                s for s in ((0, "wqT", "wqiT", "qh", False), (1, "wkT", "wkiT", "kh", False),
                            (2, "wvT", "wviT", "vh", True)) if s[3][0] in stages):
            wh = ctx.dram_in(wn, [D, D], BF16)
            wih = ctx.dram_in(win, [D, D], BF16)
            ctx.sp.wait(ctx.sy)
            wload.inc(ctx.sy.dma_start(out=wbuf[:], in_=wh[:, :].rearrange("(kc p) m -> p kc m", p=128)), 16)
            wload.inc(ctx.sy.dma_start(out=wibuf[:], in_=wih[:, :].rearrange("(kc p) m -> p kc m", p=128)), 16)
            wload.wait(ctx.te)
            bi = ctx.in_vec("b" + win, D)
            ctx.ld.wait(ctx.sc)
            if tokmajor:
                brd = ctx.dram_in("bvi_row", [1, D], F32)
                btok = ctx.sb("btok", [128, D], F32)
                ctx.ld.inc(ctx.sy.dma_start(out=btok[:],
                                            in_=brd[0:1, :].to_broadcast([128, D])), 16)
                ctx.ld.wait(ctx.v)
                hbm = ctx.dram_out(outname, [TC, D], BF16)
            else:
                hbm = ctx.dram_out(outname, [D, TC], BF16)
            for tt in range(TC // NT):
                t0 = tt * NT
                ctx.st.wait(ctx.sc)
                ctx.st.wait(ctx.v)
                for mc in range(8):
                    ctx.ss.wait(te)
                    ctx.sv.wait(te)
                    for kc2 in range(8):
                        inst = te.matmul(ps_mm[:, :], wbuf[:, kc2, mc * 128:(mc + 1) * 128],
                                         xn2b[:, kc2, t0:t0 + NT], start=(kc2 == 0), stop=(kc2 == 7))
                    ctx.sp.inc(inst)
                    ctx.sp.wait(sc)
                    ctx.ss.inc(sc.activation(eo[:, mc, :], ps_mm[:, :], AF.Copy))
                ctx.ss.wait(v)
                for c in range(4):
                    cs = cost[:, c, t0:t0 + NT]
                    sn = sint[:, c, t0:t0 + NT]
                    v.tensor_tensor(rot[:, c, :], eo[:, c, :], cs, ALU.mult)
                    v.tensor_tensor(an2[:, :], eo[:, c, :], sn, ALU.mult)
                    v.tensor_tensor(ang[:, :], eo[:, c + 4, :], sn, ALU.mult)
                    v.tensor_tensor(rot[:, c, :], rot[:, c, :], ang[:, :], ALU.add)
                    v.tensor_tensor(ang[:, :], eo[:, c + 4, :], cs, ALU.mult)
                    ins = v.tensor_tensor(rot[:, c + 4, :], ang[:, :], an2[:, :],
                                          ALU.subtract)
                ctx.sv.inc(ins)
                for mc in range(8):
                    ctx.ss.wait(te)
                    ctx.sv.wait(te)
                    if tokmajor:
                        for sub in range(4):
                            ctx.sv.wait(te)
                            for kc2 in range(8):
                                inst = te.matmul(
                                    ps_mm[0:128, 0:128], rot[:, kc2, sub * 128:(sub + 1) * 128],
                                    wibuf[:, kc2, mc * 128:(mc + 1) * 128],
                                    start=(kc2 == 0), stop=(kc2 == 7))
                            ctx.sp.inc(inst)
                            ctx.sp.wait(v)
                            evs = ctx.evac_scr()
                            v.tensor_copy(evs[:, 0:128], ps_mm[0:128, 0:128])
                            ctx.sv.inc(v.tensor_tensor(
                                otile[:, mc, sub * 128:(sub + 1) * 128], evs[:, 0:128],
                                btok[:, mc * 128:(mc + 1) * 128], ALU.add))
                    else:
                        for kc2 in range(8):
                            inst = te.matmul(ps_mm[:, :], wibuf[:, kc2, mc * 128:(mc + 1) * 128],
                                             rot[:, kc2, :], start=(kc2 == 0), stop=(kc2 == 7))
                        ctx.sp.inc(inst)
                        ctx.sp.wait(v)
                        evs = ctx.evac_scr()
                        v.tensor_copy(evs[:, :], ps_mm[:, :])
                        ctx.sv.inc(v.tensor_scalar(otile[:, mc, :], evs[:, :],
                                                   bi[:, mc:mc + 1], None, ALU.add))
                ctx.sv.wait(ctx.nc.gpsimd)
                if tokmajor:
                    for sub in range(4):
                        ctx.st.inc(ctx.gp.dma_start(
                            out=hbm[t0 + sub * 128:t0 + sub * 128 + 128, :]
                                .rearrange("p (mc f) -> p mc f", f=128),
                            in_=otile[:, :, sub * 128:(sub + 1) * 128]), 16)
                else:
                    ctx.st.inc(ctx.gp.dma_start(
                        out=hbm[:, t0:t0 + NT].rearrange("(mc p) t -> p mc t", p=128),
                        in_=otile[:]), 16)
        ctx.finish()
    return ctx



def build_l2b():
    ctx = Ctx("l2b")
    nc = ctx.nc
    L = 8192
    KB = 8
    with nc.Block():
        oc, orow = ones_consts(ctx)
        qh = ctx.sb("qh", [64, L], BF16)
        kh = ctx.sb("kh", [64, L], BF16)
        vh = ctx.sb("vh", [128, 64, 65], BF16)
        sq = ctx.sb("sq", [64, NT], BF16)
        nrm = ctx.sb("nrm", [1, NT], F32)
        mx = ctx.sb("mx", [1, 2], F32)
        mhb = ctx.sb("mhb", [1, 1], BF16)
        pt = ctx.sb("pt", [128, 8 * 128], BF16)
        avs = ctx.sb("avs", [128, 65], F32)
        avt = ctx.sb("avt", [128, 65], F32)
        rl = ctx.sb("rl", [128, 1], F32)
        ot = ctx.sb("ot", [128, 64], BF16)
        ps_s = ctx.psum("ps_s", [128, 2 * NT])
        ps_av = ctx.psum("ps_av", [128, 65])
        ps_n = ctx.psum("ps_n", [1, NT])
        ps_m = ctx.psum("ps_m", [128, 2])
        masks = np.zeros((128, 4 * 128), dtype=np.float32)
        for j in range(4):
            for p_ in range(128):
                masks[p_, j * 128:(j + 1) * 128] = (np.arange(128) >= (p_ + 0))
        dmask = ctx.const_sb("dmask", masks[:, 0:128], F32)
        ld2 = SemC(nc, "ld2")
        for p in range(4):
            qd = ctx.dram_in(f"q{p}", [64, L], BF16)
            kd = ctx.dram_in(f"k{p}", [64, L], BF16)
            vd = ctx.dram_in(f"v{p}", [L, 64], BF16)
            od = ctx.dram_out(f"o{p}", [L, 64], BF16)
            ctx.sp.wait(ctx.sy)
            ctx.sv.wait(ctx.sy)
            ld2.inc(ctx.sy.dma_start(out=qh[:], in_=qd[:, :]), 16)
            ld2.inc(ctx.sy.dma_start(out=kh[:], in_=kd[:, :]), 16)
            ld2.inc(ctx.sy.dma_start(out=vh[:, :, 0:64],
                                     in_=vd[:, :].rearrange("(c p) f -> p c f", p=128)), 16)
            ld2.wait(ctx.v)
            ld2.wait(ctx.te)
            v, sc, te = ctx.v, ctx.sc, ctx.te
            v.memset(vh[:, :, 64:65], 1.0)
            v.memset(mx[:, :], 0.0)
            for j, src_ in enumerate((qh, kh)):
                for t0 in range(0, L, NT):
                    ctx.sp.wait(v)
                    v.tensor_tensor(sq[:, :], src_[:, t0:t0 + NT], src_[:, t0:t0 + NT],
                                    ALU.mult)
                    ctx.sv.wait(te)
                    ctx.ss.wait(te)
                    inst = te.matmul(ps_n[0:1, :], oc[0:64, 0:1], sq[:, :],
                                     start=True, stop=True)
                    ctx.sp.inc(inst)
                    ctx.sp.wait(v)
                    v.tensor_copy(nrm[:, :], ps_n[0:1, :])
                    v.tensor_reduce(nrm[0:1, 0:1], nrm[0:1, :].unsqueeze(1), AX.X, ALU.max)
                    v.tensor_tensor(mx[0:1, j:j + 1], mx[0:1, j:j + 1], nrm[0:1, 0:1],
                                    ALU.max)
            v.tensor_tensor(mx[0:1, 0:1], mx[0:1, 0:1], mx[0:1, 1:2], ALU.add)
            v.tensor_scalar_mul(mx[0:1, 0:1], mx[0:1, 0:1], -1.01 / 16.0)
            v.tensor_copy(mhb[0:1, 0:1], mx[0:1, 0:1])
            ctx.sv.wait(te)
            inst = te.matmul(ps_m[:, 0:1], orow[0:1, :], mhb[0:1, 0:1], start=True, stop=True)
            ctx.sp.inc(inst)
            ctx.sp.wait(v)
            negmh = ctx.sb(f"negmh{p}", [128, 1], F32)
            v.tensor_copy(negmh[:, :], ps_m[:, 0:1])
            for qt in range(L // 128):
                q0 = qt * 128
                nkb = qt + 1
                for kb0 in range(0, nkb, KB):
                    nb = min(KB, nkb - kb0)
                    ctx.ss.wait(te)
                    ctx.sv.wait(te)
                    for b in range(nb):
                        kb = kb0 + b
                        inst = te.matmul(ps_s[:, b * 128:(b + 1) * 128],
                                         kh[:, kb * 128:(kb + 1) * 128],
                                         qh[:, q0:q0 + 128], start=True, stop=True)
                    ctx.sp.inc(inst)
                    ctx.sp.wait(sc)
                    ctx.ss.inc(sc.activation(pt[:, 0:nb * 128], ps_s[:, 0:nb * 128],
                                             AF.Exp, bias=negmh[:, 0:1], scale=0.125))
                    if kb0 + nb == nkb:
                        ctx.ss.wait(v)
                        ctx.sv.inc(v.tensor_tensor(pt[:, (nb - 1) * 128:nb * 128],
                                                   pt[:, (nb - 1) * 128:nb * 128],
                                                   dmask[:, 0:128], ALU.mult))
                    ctx.ss.wait(te)
                    ctx.sv.wait(te)
                    for b in range(nb):
                        inst = te.matmul(ps_av[:, :], pt[:, b * 128:(b + 1) * 128],
                                         vh[:, kb0 + b, :], start=(b == 0), stop=(b == nb - 1))
                    ctx.sp.inc(inst)
                    ctx.sp.wait(v)
                    if kb0 == 0:
                        ctx.sv.inc(v.tensor_copy(avs[:, :], ps_av[:, :]))
                    else:
                        v.tensor_copy(avt[:, :], ps_av[:, :])
                        ctx.sv.inc(v.tensor_tensor(avs[:, :], avs[:, :], avt[:, :],
                                                   ALU.add))
                v.reciprocal(rl[:, :], avs[:, 64:65])
                v.tensor_scalar(ot[:, :], avs[:, 0:64], rl[:, 0:1], None, ALU.mult)
                ctx.sv.wait(ctx.nc.gpsimd)
                ctx.st.inc(ctx.gp.dma_start(out=od[q0:q0 + 128, :], in_=ot[:, :]), 16)
                ctx.st.wait(v)
        ctx.finish()
    return ctx



def build_l2c():
    ctx = Ctx("l2c")
    nc = ctx.nc
    with nc.Block():
        o = ctx.in_act("o", D, TC, BF16)
        gtop = ctx.in_act("g2top", D, TC, BF16)
        wo = ctx.in_weight("out_wT", D, D)
        bo = ctx.in_vec("out_b", D)
        x2 = ctx.in_act("x2", D, TC, F32)
        x3h = ctx.dram_out("x3", [D, TC], F32)
        x3t = ctx.sb("x3t", [128, 8, NT], F32)
        scr = ctx.sb("scr", [128, NT], F32)
        ps_mm = ctx.psum("ps_mm", [128, NT])
        ctx.ld.wait(ctx.v)
        ctx.ld.wait(ctx.te)
        ctx.ld.wait(ctx.sc)
        for tt in range(TC // NT):
            t0 = tt * NT
            ctx.st.wait(ctx.v)
            for mc in range(8):
                ctx.sv.wait(ctx.te)
                for i in range(8):
                    inst = ctx.te.matmul(ps_mm[:, :], wo[:, i, mc * 128:(mc + 1) * 128],
                                         o[:, i, t0:t0 + NT], start=(i == 0), stop=(i == 7))
                ctx.sp.inc(inst)
                ctx.sp.wait(ctx.v)
                ctx.v.tensor_copy(scr[:, :], ps_mm[:, :])
                ctx.v.tensor_scalar(scr[:, :], scr[:, :], bo[:, mc:mc + 1], None, ALU.add)
                ctx.v.tensor_tensor(scr[:, :], scr[:, :], gtop[:, mc, t0:t0 + NT], ALU.mult)
                ctx.sv.inc(ctx.v.tensor_tensor(x3t[:, mc, :], scr[:, :],
                                               x2[:, mc, t0:t0 + NT], ALU.add))
            ctx.sv.wait(ctx.nc.gpsimd)
            ctx.st.inc(ctx.gp.dma_start(
                out=x3h[:, t0:t0 + NT].rearrange("(dc p) t -> p dc t", p=128),
                in_=x3t[:]), 16)
        ctx.finish()
    return ctx



HALO = 8
TEXT = TC + 2 * HALO


def build_l3a1():
    ctx = Ctx("l3a1")
    nc = ctx.nc
    with nc.Block():
        oc, orow = ones_consts(ctx)
        xh = ctx.dram_in("x3_ext", [D, TEXT], F32)
        xt = ctx.sb("xt", [128, 8, NT], F32)
        xnb = ctx.sb("xnb", [128, 8, TEXT], BF16)
        xnf = ctx.sb("xnf", [128, 8, NT], F32)
        xno = ctx.dram_out("xn3", [D, TC], F32)
        ps_small = ctx.psum("ps_small", [1, NT])
        ps_bc = ctx.psum("ps_bc", [128, 2 * NT])
        xld = SemC(nc, "xld")

        class TileView:
            def __getitem__(self, idx):
                _, c, tsl = idx
                return xt[:, c, tsl.start % NT if tsl.start % NT else 0:
                          (tsl.start % NT if tsl.start % NT else 0) + (tsl.stop - tsl.start)]

        def loader(t0, n):
            ctx.sv.wait(ctx.sy)
            ctx.sp.wait(ctx.sy)
            xld.inc(ctx.sy.dma_start(
                out=xt[:, :, 0:n],
                in_=xh[:, t0:t0 + n].rearrange("(dc p) t -> p dc t", p=128)), 16)
            xld.wait(ctx.nc.vector)
            return TileView()

        def cb(c, t0, n, xs):
            ctx.v.tensor_copy(xnb[:, c, t0:t0 + n], xs)
            lo = max(t0, HALO)
            hi = min(t0 + n, HALO + TC)
            if lo < hi:
                ctx.sv.inc(ctx.v.tensor_copy(xnf[:, c, lo - t0:hi - t0], xs[:, lo - t0:hi - t0]))
                ctx.sv.wait(ctx.nc.gpsimd)
                ctx.st.inc(ctx.gp.dma_start(
                    out=xno[c * 128:(c + 1) * 128, lo - HALO:hi - HALO],
                    in_=xnf[:, c, lo - t0:hi - t0]), 16)
                ctx.st.wait(ctx.nc.vector)
        emit_ln(ctx, TEXT, 8, None, cb, oc=oc, orow=orow, ps_small=ps_small,
                ps_bc=ps_bc, in_loader=loader)
        ctx.out_act("xn3b_ext", D, TEXT, BF16, xnb)
        ctx.finish()
    return ctx


def build_l3a2():
    ctx = Ctx("l3a2")
    nc = ctx.nc
    H2 = 2048
    with nc.Block():
        xnb = ctx.in_act("xn3b_ext", D, TEXT, BF16)
        wh = ctx.dram_in("W_inT", [D, 2 * H2], BF16)
        cw = ctx.dram_in("conv_w", [H2, 3], F32)
        cb_ = ctx.dram_in("conv_b", [128, 16], F32)
        ucod = ctx.dram_out("uc", [H2, TC], BF16)
        zod = ctx.dram_out("z_sil", [H2, TC], BF16)
        wbuf = ctx.sb("wbuf", [128, 8, H2], BF16)
        u = ctx.sb("u", [128, 16, TEXT], BF16)
        cwf = ctx.sb("cwf", [128, 16, 3], F32)
        cbf = ctx.sb("cbf", [128, 16], F32)
        accv = ctx.sb("accv", [128, NT], F32)
        uct = ctx.sb("uct", [128, NT], BF16)
        ps_mm = ctx.psum("ps_mm", [128, NT])
        ctx.ld.inc(ctx.sy.dma_start(out=cwf[:], in_=cw[:, :].rearrange("(c p) k -> p c k", p=128)), 16)
        ctx.ld.inc(ctx.sy.dma_start(out=cbf[:], in_=cb_[:, :]), 16)
        wl = SemC(nc, "wl")
        wl.inc(ctx.sy.dma_start(out=wbuf[:], in_=wh[:, 0:H2].rearrange("(kc p) m -> p kc m", p=128)), 16)
        wl.wait(ctx.te)
        ctx.ld.wait(ctx.te)
        ctx.ld.wait(ctx.sc)
        ctx.ld.wait(ctx.v)
        v, sc, te = ctx.v, ctx.sc, ctx.te
        for mc in range(16):
            for t0 in range(0, TEXT, NT):
                n = min(NT, TEXT - t0)
                ctx.sv.wait(te)
                ctx.ss.wait(te)
                for i in range(8):
                    inst = te.matmul(ps_mm[:, 0:n], wbuf[:, i, mc * 128:(mc + 1) * 128],
                                     xnb[:, i, t0:t0 + n], start=(i == 0), stop=(i == 7))
                ctx.sp.inc(inst)
                ctx.sp.wait(sc)
                ctx.ss.inc(sc.activation(u[:, mc, t0:t0 + n], ps_mm[:, 0:n], AF.Copy))
        ctx.ss.wait(v)
        for mc in range(16):
            for t0 in range(0, TC, NT):
                e0 = t0 + HALO
                ctx.st.wait(v)
                v.tensor_scalar(accv[:, :], u[:, mc, e0 - 1:e0 - 1 + NT],
                                cwf[:, mc, 0:1], None, ALU.mult)
                v.scalar_tensor_tensor(accv[:, :], u[:, mc, e0:e0 + NT],
                                       cwf[:, mc, 1:2], accv[:, :], ALU.mult, ALU.add)
                ctx.sv.inc(v.scalar_tensor_tensor(accv[:, :], u[:, mc, e0 + 1:e0 + 1 + NT],
                                                  cwf[:, mc, 2:3], accv[:, :], ALU.mult, ALU.add))
                ctx.sv.wait(sc)
                ctx.ss.inc(sc.activation(uct[:, :], accv[:, :], AF.Silu,
                                         bias=cbf[:, mc:mc + 1]))
                ctx.ss.wait(ctx.nc.gpsimd)
                ctx.st.inc(ctx.gp.dma_start(out=ucod[mc * 128:(mc + 1) * 128, t0:t0 + NT],
                                            in_=uct[:, :]), 16)
        ctx.sp.wait(ctx.sy)
        wl.inc(ctx.sy.dma_start(out=wbuf[:], in_=wh[:, H2:2 * H2].rearrange("(kc p) m -> p kc m", p=128)), 16)
        wl.wait(ctx.te)
        for mc in range(16):
            for t0 in range(0, TC, NT):
                ctx.sv.wait(te)
                ctx.ss.wait(te)
                ctx.st.wait(ctx.sc)
                for i in range(8):
                    inst = te.matmul(ps_mm[:, :], wbuf[:, i, mc * 128:(mc + 1) * 128],
                                     xnb[:, i, HALO + t0:HALO + t0 + NT],
                                     start=(i == 0), stop=(i == 7))
                ctx.sp.inc(inst)
                ctx.sp.wait(sc)
                ctx.ss.inc(sc.activation(uct[:, :], ps_mm[:, :], AF.Silu))
                ctx.ss.wait(ctx.nc.gpsimd)
                ctx.st.inc(ctx.gp.dma_start(out=zod[mc * 128:(mc + 1) * 128, t0:t0 + NT],
                                            in_=uct[:, :]), 16)
        ctx.finish()
    return ctx


def build_l3a3():
    ctx = Ctx("l3a3")
    nc = ctx.nc
    H2 = 2048
    with nc.Block():
        uc = ctx.in_act("uc", H2, TC, BF16)
        wxd = ctx.in_weight("WxdT", H2, 64)
        wxb = ctx.in_weight("WxbT", H2, 16)
        wxc = ctx.in_weight("WxcT", H2, 16)
        wdt = ctx.in_weight("WdtT", 64, H2)
        bdt = ctx.in_vec("b_dt", H2)
        dtp = ctx.sb("dtp", [64, 1, TC], BF16)
        bc_s = ctx.sb("bc_s", [16, TC], BF16)
        dtt = ctx.sb("dtt", [128, NT], BF16)
        t1 = ctx.sb("t1", [128, NT], F32)
        t2 = ctx.sb("t2", [128, NT], F32)
        dto = ctx.dram_out("dt", [H2, TC], BF16)
        ps_mm = ctx.psum("ps_mm", [128, NT])
        ctx.ld.wait(ctx.te)
        ctx.ld.wait(ctx.sc)
        ctx.ld.wait(ctx.v)
        v, sc, te = ctx.v, ctx.sc, ctx.te
        for t0 in range(0, TC, NT):
            ctx.ss.wait(te)
            for i in range(16):
                inst = te.matmul(ps_mm[0:64, :], wxd[:, i, :], uc[:, i, t0:t0 + NT],
                                 start=(i == 0), stop=(i == 15))
            ctx.sp.inc(inst)
            ctx.sp.wait(sc)
            ctx.ss.inc(sc.activation(dtp[:, 0, t0:t0 + NT], ps_mm[0:64, :], AF.Copy))
        for nm, w_, row in (("Bm", wxb, 0), ("Cm", wxc, 1)):
            hb = ctx.dram_out(nm, [16, TC], BF16)
            for t0 in range(0, TC, NT):
                ctx.ss.wait(te)
                for i in range(16):
                    inst = te.matmul(ps_mm[0:16, :], w_[:, i, :], uc[:, i, t0:t0 + NT],
                                     start=(i == 0), stop=(i == 15))
                ctx.sp.inc(inst)
                ctx.sp.wait(sc)
                ctx.ss.inc(sc.activation(bc_s[:, t0:t0 + NT], ps_mm[0:16, :], AF.Copy))
            ctx.ss.wait(ctx.nc.gpsimd)
            ctx.st.inc(ctx.gp.dma_start(out=hb[:, :], in_=bc_s[:, :]), 16)
            ctx.st.wait(ctx.sc)
        for mc in range(16):
            for t0 in range(0, TC, NT):
                ctx.sv.wait(te)
                ctx.ss.wait(te)
                inst = te.matmul(ps_mm[:, :], wdt[0:64, 0, mc * 128:(mc + 1) * 128],
                                 dtp[:, 0, t0:t0 + NT], start=True, stop=True)
                ctx.sp.inc(inst)
                ctx.sp.wait(v)
                v.tensor_copy(t1[:, :], ps_mm[:, :])
                ctx.sv.inc(v.tensor_scalar(t1[:, :], t1[:, :], bdt[:, mc:mc + 1], None,
                                           ALU.add))
                ctx.sv.wait(sc)
                sc.activation(t2[:, :], t1[:, :], AF.Abs)
                sc.activation(t2[:, :], t2[:, :], AF.Exp, scale=-1.0)
                ctx.ss.inc(sc.activation(t2[:, :], t2[:, :], AF.Ln, bias=ctx.cap(1.0)))
                ctx.ss.wait(v)
                v.tensor_scalar_max(t1[:, :], t1[:, :], 0.0)
                ctx.sv.inc(v.tensor_tensor(t1[:, :], t1[:, :], t2[:, :], ALU.add))
                ctx.st.wait(ctx.v)
                ctx.sv.inc(ctx.v.tensor_copy(dtt[:, :], t1[:, :]))
                ctx.sv.wait(ctx.nc.gpsimd)
                ctx.st.inc(ctx.gp.dma_start(out=dto[mc * 128:(mc + 1) * 128, t0:t0 + NT],
                                            in_=dtt[:, :]), 16)
        ctx.finish()
    return ctx


def build_l3f1():
    ctx = Ctx("l3f1")
    nc = ctx.nc
    with nc.Block():
        xin = ctx.dram_in("xn3_tok", [TC, D], F32)
        xout = ctx.dram_out("xq_tok", [TC, D], BF16)
        qt = dict(
            x=ctx.sb("qx", [128, D], F32), xa=ctx.sb("qxa", [128, D], F32),
            s=ctx.sb("qs", [128, 64], F32), rs=ctx.sb("qrs", [128, 64], F32),
            lo=ctx.sb("qlo", [128, D], F32), ti=ctx.sb("qti", [128, D], I32),
            t1=ctx.sb("qt1", [128, D], F32), sg=ctx.sb("qsg", [128, D], F32),
            o=ctx.sb("qo", [128, D], BF16),
        )
        for r0 in range(0, TC, 128):
            ctx.sv.wait(ctx.sy)
            ctx.ss.wait(ctx.sy)
            ctx.ld.inc(ctx.sy.dma_start(out=qt["x"][:, :], in_=xin[r0:r0 + 128, :]), 16)
            ctx.ld.wait(ctx.v)
            ctx.ld.wait(ctx.sc)
            ctx.st.wait(ctx.v)
            emit_fp4(ctx, qt, 128, D)
            ctx.sv.wait(ctx.nc.gpsimd)
            ctx.st.inc(ctx.gp.dma_start(out=xout[r0:r0 + 128, :], in_=qt["o"][:, :]), 16)
        ctx.finish()
    return ctx


def build_l3f2():
    ctx = Ctx("l3f2")
    nc = ctx.nc
    IF = 2816
    with nc.Block():
        xq = ctx.in_act("xq_fm", D, TC, BF16)
        wg = ctx.in_weight("wgq", D, IF)
        wu = ctx.in_weight("wuq", D, IF)
        hout = ctx.dram_out("hq_tok", [TC, IF], BF16)
        hrow = ctx.sb("hrow", [128, IF], F32)
        sg_ = ctx.sb("sg_", [128, NT], F32)
        qt = dict(
            xa=ctx.sb("qxa", [128, IF], F32),
            s=ctx.sb("qs", [128, IF // 16], F32), rs=ctx.sb("qrs", [128, IF // 16], F32),
            lo=ctx.sb("qlo", [128, IF], F32), ti=ctx.sb("qti", [128, IF], I32),
            t1=ctx.sb("qt1", [128, IF], F32), sg=ctx.sb("qsg", [128, IF], F32),
            o=ctx.sb("qo", [128, IF], BF16),
        )
        qt["x"] = hrow
        ps_g = ctx.psum("ps_g", [128, NT])
        ps_u = ctx.psum("ps_u", [128, NT])
        ctx.ld.wait(ctx.te)
        ctx.ld.wait(ctx.sc)
        ctx.ld.wait(ctx.v)
        v, sc, te = ctx.v, ctx.sc, ctx.te
        for rt in range(TC // 128):
            r0 = rt * 128
            for f0 in range(0, IF, NT):
                n = min(NT, IF - f0)
                ctx.sv.wait(te)
                ctx.ss.wait(te)
                for i in range(8):
                    te.matmul(ps_g[:, 0:n], xq[:, i, r0:r0 + 128], wg[:, i, f0:f0 + n],
                              start=(i == 0), stop=(i == 7))
                for i in range(8):
                    inst = te.matmul(ps_u[:, 0:n], xq[:, i, r0:r0 + 128], wu[:, i, f0:f0 + n],
                                     start=(i == 0), stop=(i == 7))
                ctx.sp.inc(inst)
                ctx.sp.wait(sc)
                ctx.ss.inc(sc.activation(sg_[:, 0:n], ps_g[:, 0:n], AF.Silu))
                ctx.ss.wait(v)
                v.tensor_copy(hrow[:, f0:f0 + n], ps_u[:, 0:n])
                ctx.sv.inc(v.tensor_tensor(hrow[:, f0:f0 + n], hrow[:, f0:f0 + n],
                                           sg_[:, 0:n], ALU.mult))
            ctx.st.wait(v)
            ctx.st.wait(ctx.sc)
            emit_fp4(ctx, qt, 128, IF)
            ctx.sv.wait(ctx.nc.gpsimd)
            ctx.st.inc(ctx.gp.dma_start(out=hout[r0:r0 + 128, :], in_=qt["o"][:, :]), 16)
        ctx.finish()
    return ctx


def build_l3f3():
    ctx = Ctx("l3f3")
    nc = ctx.nc
    IF = 2816
    with nc.Block():
        hq = ctx.in_act("hq_fm", IF, TC, BF16)
        wd = ctx.in_weight("wdq", IF, D)
        gout = ctx.dram_out("g3top", [D, TC], BF16)
        gt = ctx.sb("gt", [128, NT], BF16)
        ps_mm = ctx.psum("ps_mm", [128, NT])
        ctx.ld.wait(ctx.te)
        ctx.ld.wait(ctx.sc)
        for mc in range(8):
            for t0 in range(0, TC, NT):
                ctx.ss.wait(ctx.te)
                for i in range(22):
                    inst = ctx.te.matmul(ps_mm[:, :], wd[:, i, mc * 128:(mc + 1) * 128],
                                         hq[:, i, t0:t0 + NT], start=(i == 0), stop=(i == 21))
                ctx.sp.inc(inst)
                ctx.sp.wait(ctx.sc)
                ctx.st.wait(ctx.sc)
                ctx.ss.inc(ctx.sc.activation(gt[:, :], ps_mm[:, :], AF.Silu))
                ctx.ss.wait(ctx.nc.gpsimd)
                ctx.st.inc(ctx.gp.dma_start(out=gout[mc * 128:(mc + 1) * 128, t0:t0 + NT],
                                            in_=gt[:, :]), 16)
        ctx.finish()
    return ctx


def build_l3b():
    ctx = Ctx("l3b")
    nc = ctx.nc
    L = 8192
    TT_ = 2048
    with nc.Block():
        uc = ctx.in_act("uc", 512, L, BF16)
        dt = ctx.in_act("dt", 512, L, BF16)
        bmd = ctx.dram_in("Bm", [16, L], BF16)
        cmd = ctx.dram_in("Cm", [16, L], BF16)
        zsd = ctx.dram_in("z_sil", [512, L], BF16)
        dd = ctx.dram_in("Dvec", [128, 4], F32)
        alogd = ctx.dram_in("A_log", [128, 16], F32)
        y2o = ctx.dram_out("y2", [512, L], BF16)
        dvec = ctx.sb("dvec", [128, 4], F32)
        alog = ctx.sb("alog_s", [128, 16], F32)
        av = ctx.sb("av", [128, 16], F32)
        bbc = ctx.sb("bbc", [128, TT_], BF16)
        cbc = ctx.sb("cbc", [128, TT_], BF16)
        dA = ctx.sb("dA", [128, TT_], BF16)
        d1 = ctx.sb("d1", [128, TT_], BF16)
        hsc = ctx.sb("hsc", [128, TT_], F32)
        ytile = ctx.sb("ytile", [128, 4, TT_], F32)
        carry = ctx.sb("carry", [128, 4, 16], F32)
        zst = ctx.sb("zst", [128, TT_], BF16)
        scr = ctx.sb("scr3", [128, TT_], F32)
        yo = ctx.sb("yo", [128, TT_], BF16)
        ctx.ld.inc(ctx.sy.dma_start(out=dvec[:], in_=dd[:, :]), 16)
        ctx.ld.inc(ctx.sy.dma_start(out=alog[:], in_=alogd[:, :]), 16)
        ctx.ld.wait(ctx.v)
        ctx.ld.wait(ctx.sc)
        v, sc = ctx.v, ctx.sc
        sc.activation(av[:, :], alog[:, :], AF.Exp)
        v.tensor_scalar_mul(av[:, :], av[:, :], -1.0)
        for c4 in range(4):
            v.memset(carry[:, c4, :], 0.0)
        bsem = SemC(nc, "bsem")
        for tt in range(L // TT_):
            t0 = tt * TT_
            for ct in range(4):
                v.memset(ytile[:, ct, :], 0.0)
            for n in range(16):
                ctx.sv.wait(ctx.sy)
                bsem.inc(ctx.sy.dma_start(
                    out=bbc[:, :], in_=bmd[n:n + 1, t0:t0 + TT_].to_broadcast([128, TT_])), 16)
                bsem.inc(ctx.sy.dma_start(
                    out=cbc[:, :], in_=cmd[n:n + 1, t0:t0 + TT_].to_broadcast([128, TT_])), 16)
                bsem.wait(ctx.nc.vector)
                bsem.wait(ctx.nc.scalar)
                for ct in range(4):
                    ctx.sv.wait(sc)
                    ctx.ss.inc(sc.activation(dA[:, :], dt[:, ct, t0:t0 + TT_], AF.Exp,
                                             scale=av[:, n:n + 1]))
                    ctx.ss.wait(v)
                    v.tensor_tensor(d1[:, :], uc[:, ct, t0:t0 + TT_], bbc[:, :], ALU.mult)
                    v.tensor_tensor_scan(hsc[:, :], dA[:, :], d1[:, :],
                                         carry[:, ct, n:n + 1], ALU.mult, ALU.add)
                    v.tensor_copy(carry[:, ct, n:n + 1], hsc[:, TT_ - 1:TT_])
                    v.tensor_tensor(scr[:, :], hsc[:, :], cbc[:, :], ALU.mult)
                    ctx.sv.inc(v.tensor_tensor(ytile[:, ct, :], ytile[:, ct, :],
                                               scr[:, :], ALU.add))
            for ct in range(4):
                ctx.sv.wait(ctx.sy)
                bsem.inc(ctx.sy.dma_start(
                    out=zst[:, :], in_=zsd[ct * 128:(ct + 1) * 128, t0:t0 + TT_]), 16)
                bsem.wait(ctx.nc.vector)
                v.scalar_tensor_tensor(scr[:, :], uc[:, ct, t0:t0 + TT_],
                                       dvec[:, ct:ct + 1], ytile[:, ct, :],
                                       ALU.mult, ALU.add)
                ctx.st.wait(v)
                ctx.sv.inc(v.tensor_tensor(yo[:, :], scr[:, :], zst[:, :], ALU.mult))
                ctx.sv.wait(ctx.nc.gpsimd)
                ctx.st.inc(ctx.gp.dma_start(out=y2o[ct * 128:(ct + 1) * 128, t0:t0 + TT_],
                                            in_=yo[:, :]), 16)
        ctx.finish()
    return ctx


def build_l3c():
    ctx = Ctx("l3c")
    nc = ctx.nc
    H2 = 2048
    with nc.Block():
        y2 = ctx.in_act("y2", H2, TC, BF16)
        wo = ctx.in_weight("W_outT", H2, D)
        xn3d = ctx.dram_in("xn3", [D, TC], F32)
        gtopd = ctx.dram_in("g3top", [D, TC], BF16)
        x3d = ctx.dram_in("x3", [D, TC], F32)
        xn3 = ctx.sb("xn3t", [128, 8, NT], F32)
        gtop = ctx.sb("gtopt", [128, 8, NT], BF16)
        x3 = ctx.sb("x3t", [128, 8, NT], F32)
        x4h = ctx.dram_out("x4", [D, TC], F32)
        x4t = ctx.sb("x4t", [128, 8, NT], F32)
        scr = ctx.sb("scr", [128, NT], F32)
        ps_mm = ctx.psum("ps_mm", [128, NT])
        tl = SemC(nc, "tl")
        ctx.ld.wait(ctx.te)
        ctx.ld.wait(ctx.sc)
        ctx.ld.wait(ctx.v)
        for tt in range(TC // NT):
            t0 = tt * NT
            ctx.sv.wait(ctx.sy)
            tl.inc(ctx.sy.dma_start(out=xn3[:], in_=xn3d[:, t0:t0 + NT].rearrange("(dc p) t -> p dc t", p=128)), 16)
            tl.inc(ctx.sy.dma_start(out=gtop[:], in_=gtopd[:, t0:t0 + NT].rearrange("(dc p) t -> p dc t", p=128)), 16)
            tl.inc(ctx.sy.dma_start(out=x3[:], in_=x3d[:, t0:t0 + NT].rearrange("(dc p) t -> p dc t", p=128)), 16)
            tl.wait(ctx.nc.vector)
            ctx.st.wait(ctx.v)
            for mc in range(8):
                ctx.sv.wait(ctx.te)
                for i in range(16):
                    inst = ctx.te.matmul(ps_mm[:, :], wo[:, i, mc * 128:(mc + 1) * 128],
                                         y2[:, i, t0:t0 + NT], start=(i == 0), stop=(i == 15))
                ctx.sp.inc(inst)
                ctx.sp.wait(ctx.v)
                v = ctx.v
                v.tensor_copy(scr[:, :], ps_mm[:, :])
                v.tensor_tensor(scr[:, :], scr[:, :], xn3[:, mc, :], ALU.add)
                v.tensor_tensor(scr[:, :], scr[:, :], gtop[:, mc, :], ALU.mult)
                ctx.sv.inc(v.tensor_tensor(x4t[:, mc, :], scr[:, :],
                                           x3[:, mc, :], ALU.add))
            ctx.sv.wait(ctx.nc.gpsimd)
            ctx.st.inc(ctx.gp.dma_start(
                out=x4h[:, t0:t0 + NT].rearrange("(dc p) t -> p dc t", p=128),
                in_=x4t[:]), 16)
        ctx.finish()
    return ctx


def run_layer2(X2, W, keep=None):
    keep = keep if keep is not None else {}
    x2s = _slices(X2)
    ctx = build_l2a()
    im = []
    for c in range(NCORES):
        pos = np.tile((np.arange(TC, dtype=np.float32) + (c % 4) * TC)[None, :], (128, 1))
        im.append(dict(x2=x2s[c], pos=np.ascontiguousarray(pos),
                       bvi_row=W["l2_bvi_row"],
                       ln_g=W["l2_lng"], ln_b=W["l2_lnb"],
                       wqT=W["l2_wqT"], wqiT=W["l2_wqiT"], bwqiT=W["l2_bqi"],
                       wkT=W["l2_wkT"], wkiT=W["l2_wkiT"], bwkiT=W["l2_bki"],
                       wvT=W["l2_wvT"], wviT=W["l2_wviT"], bwviT=W["l2_bvi"]))
    r = _run(ctx, im)
    xn2 = _gather(r, "xn2b")
    g2top = _gather(r, "g2top")
    qh = _gather(r, "qh")
    kh = _gather(r, "kh")
    vh_tok = np.concatenate([r[c]["vh"] for c in range(NCORES)], axis=0)
    keep.update(xn2=xn2, g2top=g2top, qh=qh, kh=kh, vh=vh_tok)

    ctx = build_l2b()
    im = []
    for c in range(NCORES):
        m = {}
        for p in range(4):
            pr = 4 * c + p
            b, h = pr // 16, pr % 16
            sl = slice(b * 8192, (b + 1) * 8192)
            m[f"q{p}"] = np.ascontiguousarray(qh[h * 64:(h + 1) * 64, sl])
            m[f"k{p}"] = np.ascontiguousarray(kh[h * 64:(h + 1) * 64, sl])
            m[f"v{p}"] = np.ascontiguousarray(vh_tok[sl, h * 64:(h + 1) * 64])
        im.append(m)
    r = _run(ctx, im)
    o_tok = np.zeros((TTOT, D), BF)
    for c in range(NCORES):
        for p in range(4):
            pr = 4 * c + p
            b, h = pr // 16, pr % 16
            o_tok[b * 8192:(b + 1) * 8192, h * 64:(h + 1) * 64] = r[c][f"o{p}"]
    keep["o2"] = o_tok

    ctx = build_l2c()
    o_fm = _t(o_tok)
    os_ = _slices(o_fm)
    gs = _slices(g2top)
    im = [dict(o=os_[c], g2top=gs[c], out_wT=W["l2_woT"], out_b=W["l2_bo"], x2=x2s[c])
          for c in range(NCORES)]
    r = _run(ctx, im)
    x3 = _gather(r, "x3")
    return np.ascontiguousarray(x3.astype(np.float32)), keep


def run_layer3(X3, W, keep=None):
    keep = keep if keep is not None else {}
    x3s = _slices(X3)
    ctx = build_l3a1()
    im = []
    for c in range(NCORES):
        ext = np.zeros((D, TEXT), np.float32)
        g0 = c * TC - HALO
        lo = max(g0, (c // 4) * 8192)
        hi = min(c * TC + TC + HALO, (c // 4 + 1) * 8192)
        ext[:, lo - g0:hi - g0] = X3[:, lo:hi]
        im.append(dict(x3_ext=ext))
    r = _run(ctx, im)
    xnb_ext = [r[c]["xn3b_ext"] for c in range(NCORES)]
    xn3 = _gather(r, "xn3")
    keep["xn3"] = xn3

    ctx = build_l3a2()
    im = [dict(xn3b_ext=xnb_ext[c], W_inT=W["l3_WinT"], conv_w=W["l3_convw"],
               conv_b=W["l3_convb"]) for c in range(NCORES)]
    r = _run(ctx, im)
    uc = _gather(r, "uc")
    zs = _gather(r, "z_sil")
    keep.update(uc=uc, z_sil=zs)

    ctx = build_l3a3()
    ucs = _slices(uc)
    im = [dict(uc=ucs[c], WxdT=W["l3_WxdT"], WxbT=W["l3_WxbT"], WxcT=W["l3_WxcT"],
               WdtT=W["l3_WdtT"], b_dt=W["l3_bdt"]) for c in range(NCORES)]
    r = _run(ctx, im)
    dt = _gather(r, "dt")
    Bm = _gather(r, "Bm")
    Cm = _gather(r, "Cm")
    keep.update(dt=dt, Bm=Bm, Cm=Cm)

    ctx = build_l3f1()
    im = [dict(xn3_tok=_t(xn3[:, c * TC:(c + 1) * TC])) for c in range(NCORES)]
    r = _run(ctx, im)
    xq_fm = np.concatenate([_t(r[c]["xq_tok"]) for c in range(NCORES)], axis=1)
    keep["xq"] = xq_fm

    ctx = build_l3f2()
    xqs = _slices(xq_fm)
    im = [dict(xq_fm=xqs[c], wgq=W["l3_wgq"], wuq=W["l3_wuq"]) for c in range(NCORES)]
    r = _run(ctx, im)
    hq_fm = np.concatenate([_t(r[c]["hq_tok"]) for c in range(NCORES)], axis=1)
    keep["hq"] = hq_fm

    ctx = build_l3f3()
    hqs = _slices(hq_fm)
    im = [dict(hq_fm=hqs[c], wdq=W["l3_wdq"]) for c in range(NCORES)]
    r = _run(ctx, im)
    g3top = _gather(r, "g3top")
    keep["g3top"] = g3top

    ctx = build_l3b()
    im = []
    for c in range(NCORES):
        b, ch0 = c // 4, (c % 4) * 512
        sl = slice(b * 8192, (b + 1) * 8192)
        im.append(dict(
            uc=np.ascontiguousarray(uc[ch0:ch0 + 512, sl]),
            dt=np.ascontiguousarray(dt[ch0:ch0 + 512, sl]),
            Bm=np.ascontiguousarray(Bm[:, sl]),
            Cm=np.ascontiguousarray(Cm[:, sl]),
            z_sil=np.ascontiguousarray(zs[ch0:ch0 + 512, sl]),
            Dvec=vec_fm(np.asarray(W["l3_D"], np.float32).ravel()[ch0:ch0 + 512]),
            A_log=W["l3_Alog"]))
    r = _run(ctx, im)
    y2 = np.zeros((2048, TTOT), BF)
    for c in range(NCORES):
        b, ch0 = c // 4, (c % 4) * 512
        y2[ch0:ch0 + 512, b * 8192:(b + 1) * 8192] = r[c]["y2"]
    keep["y2"] = y2

    ctx = build_l3c()
    y2s = _slices(y2)
    xns = _slices(xn3)
    g3s = _slices(g3top)
    im = [dict(y2=y2s[c], W_outT=W["l3_WoutT"], xn3=xns[c], g3top=g3s[c], x3=x3s[c])
          for c in range(NCORES)]
    r = _run(ctx, im)
    x4 = _gather(r, "x4")
    return np.ascontiguousarray(x4.astype(np.float32)), keep


def kernel(**inputs):
    W = host_prep(inputs)
    X = np.ascontiguousarray(np.asarray(inputs["L0_attn_x"]).reshape(TTOT, D).T)
    x1, _ = run_layer0(X, W, {})
    x2, _ = run_layer1(x1, W, {})
    x3, _ = run_layer2(x2, W, {})
    x4, _ = run_layer3(x3, W, {})
    return np.ascontiguousarray(x4.T).reshape(2, 8192, 1024).astype(np.float32)
